# version 22
# baseline (speedup 1.0000x reference)
import math
import numpy as np
import ml_dtypes

import concourse.bass as bass
import concourse.mybir as mybir
from concourse.bass_utils import run_bass_kernel_spmd

F32 = mybir.dt.float32
BF16 = mybir.dt.bfloat16
AF = mybir.ActivationFunctionType
ALU = mybir.AluOpType
AX = mybir.AxisListType

B = 2
D = 1024
H = 16
DH = 64
NP = 8
NCH = 8
HID = 2048
NE = 8
SCALE = 1.0 / math.sqrt(DH)
EPS = 1e-5
Q0KT = 4


class Rec:
    ENGS = ("sync", "tensor", "vector", "scalar", "gpsimd")
    CHAIN = {"vector": "ch_dve", "scalar": "ch_act", "gpsimd": "ch_pool",
             "tensor": "ch_pe"}

    def __init__(self):
        self.ops = {e: [] for e in self.ENGS}
        self.cnt = {}

    def sem(self, name):
        self.cnt.setdefault(name, 0)
        return name

    def val(self, s):
        return self.cnt.get(s, 0)

    def op(self, eng, fn, waits=(), incs=(), chain=False):
        w = list(waits)
        i = list(incs)
        if chain:
            ch = self.CHAIN[eng]
            self.sem(ch)
            w.append((ch, self.cnt[ch]))
            i.append((ch, 1))
        for s, _ in w:
            self.sem(s)
        for s, n in i:
            self.sem(s)
            self.cnt[s] += n
        self.ops[eng].append((w, fn, i))

    def wait(self, eng, waits):
        for s, _ in waits:
            self.sem(s)
        self.ops[eng].append((list(waits), None, []))

    def milestone(self, eng, sem="ch_pe"):
        self.sem(sem)
        for idx in range(len(self.ops[eng]) - 1, -1, -1):
            w, fn, incs = self.ops[eng][idx]
            if fn is None:
                continue
            for sname, _ in incs:
                if sname == sem:
                    return self.cnt[sem]
            if incs:
                self.cnt[sem] += 1
                self.ops[eng].append(([], lambda e: e.nop(), [(sem, 1)]))
                return self.cnt[sem]
            incs.append((sem, 1))
            self.cnt[sem] += 1
            return self.cnt[sem]
        raise RuntimeError("no real op to attach milestone")


def build(seq=8192, nblocks=None, feat=None):
    feat = feat or {}
    F_SAMPLE = feat.get("sample", True)
    F_MASK = feat.get("mask", True)
    F_EXACT = feat.get("exact", True)
    F_DANCE = feat.get("dance", True)
    F_SPARSE = feat.get("sparse", True)
    NT = seq // 128
    NW = seq // 512
    NM = NT // 8
    OWNQ = NM * 256
    OWN_QT = OWNQ // 128
    TH = OWNQ // 2
    TGS = min(512, TH)
    QB = max(1, OWNQ // 512)
    CAP = 640 if OWNQ >= 2048 else 256
    CT = CAP // 128
    ETG = [(0, min(CAP, 512))] + ([(512, CAP)] if CAP > 512 else [])
    OOB = 1 << 20

    nc = bass.Bass("TRN2", target_bir_lowering=False)

    def dram(name, shape, dtype, kind):
        return nc.dram_tensor(name, shape, dtype, kind=kind).ap()

    x_in = dram("x", [seq, D], F32, "ExternalInput")
    xo_in = dram("x_own", [OWNQ, D], F32, "ExternalInput")
    wq_in = dram("wq", [128, NP, NCH, 128], F32, "ExternalInput")
    wk_in = dram("wk", [128, NP, NCH, 128], F32, "ExternalInput")
    wv_in = dram("wv", [128, NCH, H * DH], F32, "ExternalInput")
    bq_in = dram("bq", [128, NP], F32, "ExternalInput")
    bk_in = dram("bk", [128, NP], F32, "ExternalInput")
    bv_in = dram("bv", [128, NP], F32, "ExternalInput")
    wo_in = dram("wo", [128, NCH, D], F32, "ExternalInput")
    bo_in = dram("bo", [1, D], F32, "ExternalInput")
    wg_in = dram("wg", [128, NCH, NE], F32, "ExternalInput")
    w1_in = dram("w1", [NE, 128, NCH, HID], F32, "ExternalInput")
    w3_in = dram("w3", [NE, 128, NCH, HID], F32, "ExternalInput")
    w2_in = dram("w2", [NE, 128, HID // 128, D], F32, "ExternalInput")
    scr_w1 = dram("scr_w1", [NE, 128, NCH, HID], BF16, "Internal")
    scr_w3 = dram("scr_w3", [NE, 128, NCH, HID], BF16, "Internal")
    scr_w2 = dram("scr_w2", [NE, 128, HID // 128, D], BF16, "Internal")
    pm_in = dram("pmask", [128, NM * 12 * 128], BF16, "ExternalInput")
    pm0_in = dram("pmask0", [128, Q0KT * 128], F32, "ExternalInput")

    out_t = dram("out", [OWNQ, D], F32, "ExternalOutput")

    scr_k = dram("scr_k", [NP, 128, seq], BF16, "Internal")
    scr_xnt = dram("scr_xnt", [OWNQ, D], BF16, "Internal")
    scr_cmb = dram("scr_cmb", [OWNQ, NE], F32, "Internal")
    NSLOT = NE * CAP
    SDUMP = 2 * NSLOT + 16384
    scr_gsrc = dram("scr_gsrc", [SDUMP, 1], mybir.dt.int32, "Internal")
    scr_gdst = dram("scr_gdst", [SDUMP, 1], mybir.dt.int32, "Internal")
    scr_ge = dram("scr_ge", [2 * OWNQ + 128, D], F32, "Internal")
    scr_v = dram("scr_v", [NT, 128, NP * 130], BF16, "Internal")
    scr_s = dram("scr_s", [OWN_QT, 128, D], BF16, "Internal")

    r = Rec()
    ctxs = []

    def sb(name, shape, dtype):
        cm = nc.sbuf_tensor(name, shape, dtype)
        h = cm.__enter__()
        ctxs.append([name, cm])
        return h

    def psum(name, shape, dtype):
        cm = nc.psum_tensor(name, shape, dtype)
        h = cm.__enter__()
        ctxs.append([name, cm])
        return h

    def close(*names):
        ns = set(names)
        for item in reversed(list(ctxs)):
            if item[0] in ns:
                item[1].__exit__(None, None, None)
                ctxs.remove(item)

    ident16 = sb("ident16", [128, 128], BF16)
    ident32 = sb("ident32", [128, 128], F32)
    ones16 = sb("ones16", [1, 128], BF16)
    scores = sb("scores", [128, OWN_QT * NE], F32)
    bvt = sb("bvt", [128, NP], F32)
    bqp = sb("bqp", [128, NP], F32)
    bkp = sb("bkp", [128, NP], F32)
    bo_sb = sb("bo_sb", [1, D], BF16)
    sm = sb("sm", [128, 24], F32)
    sm16 = sb("sm16", [128, 4], BF16)
    gsrc_sb = sb("gsrc_sb", [128, NE * CAP // 128], mybir.dt.int32)
    gdst_sb = sb("gdst_sb", [128, NE * CAP // 128], mybir.dt.int32)
    q_sb = sb("q_sb", [128, NP, OWNQ], BF16)
    xn_own = sb("xn_own", [128, NCH, OWNQ], BF16)

    r.op("gpsimd", lambda g: g.memset(ident16[:], 1.0), chain=True)
    r.op("gpsimd", lambda g: g.affine_select(
        ident16[:], ident16[:], [[-1, 128]], ALU.is_equal, 0.0,
        channel_multiplier=1), chain=True)
    r.op("gpsimd", lambda g: g.memset(ident32[:], 1.0), chain=True)
    r.op("gpsimd", lambda g: g.affine_select(
        ident32[:], ident32[:], [[-1, 128]], ALU.is_equal, 0.0,
        channel_multiplier=1), chain=True)
    r.op("gpsimd", lambda g: g.memset(ones16[:], 1.0), chain=True)
    consts_done = r.val("ch_pool")

    wq_sb = sb("wq_sb", [128, NP * NCH, 128], BF16)
    wk_sb = sb("wk_sb", [128, NP * NCH, 128], BF16)
    wv_sb = sb("wv_sb", [128, NCH, H * DH], BF16)
    wg_sb = sb("wg_sb", [128, NCH, NE], BF16)

    for fn in (
        lambda g: g.dma_start(
            out=wq_sb[:].rearrange("p (a c) n -> p a c n", a=NP), in_=wq_in),
        lambda g: g.dma_start(
            out=wk_sb[:].rearrange("p (a c) n -> p a c n", a=NP), in_=wk_in),
        lambda g: g.dma_start(out=wv_sb[:], in_=wv_in),
        lambda g: g.dma_start(out=wg_sb[:], in_=wg_in),
        lambda g: g.dma_start(out=bvt[:], in_=bv_in),
        lambda g: g.dma_start(out=bqp[:], in_=bq_in),
        lambda g: g.dma_start(out=bkp[:], in_=bk_in),
        lambda g: g.dma_start(out=bo_sb[:], in_=bo_in),
    ):
        r.op("gpsimd", fn, incs=[("d_w", 16)])
    w0_loads = r.val("d_w")

    x_buf = [sb(f"x_buf{i}", [128, 4, D], F32) for i in range(2)]
    xn_tok = [sb(f"xn_tok{i}", [128, D], BF16) for i in range(2)]
    xn_win = [sb(f"xn_win{i}", [128, NCH, 512], BF16) for i in range(2)]
    v_stage = [sb(f"v_stage{i}", [128, NP * 130], BF16) for i in range(2)]
    k_stage = [sb(f"k_stage{i}", [128, NP, 512], BF16) for i in range(2)]
    xsq2 = [sb(f"xsq2_{i}", [128, D], F32) for i in range(2)]

    ps_t = psum("ps_t", [128, 2048], BF16)
    ps_v = psum("ps_v", [128, 2048], F32)
    ps_k = psum("ps_k", [128, 1024], F32)

    for i in range(2):
        r.op("vector", lambda v, i=i: v.memset(v_stage[i][:], 1.0), chain=True)
    r.op("vector", lambda v: v.memset(sm[:, 23:24], EPS), chain=True)

    def ln_tile(xap, outap, pre_waits, cb=0):
        r.op("vector", lambda v: v.reduce_sum(
            sm[:, cb:cb + 1], xap(), axis=AX.X), waits=pre_waits, chain=True)
        r.op("scalar", lambda s: s.activation(
            xsq2[cb // 8][:], xap(), AF.Square,
            accum_out=sm[:, cb + 1:cb + 2]),
            waits=list(pre_waits), chain=True)
        sq = r.val("ch_act")
        r.op("vector", lambda v: v.tensor_scalar(
            sm[:, cb + 2:cb + 3], sm[:, cb:cb + 1], 1.0 / D, None, ALU.mult),
            chain=True)
        r.op("vector", lambda v: v.tensor_scalar(
            sm[:, cb + 3:cb + 4], sm[:, cb + 1:cb + 2], 1.0 / D, None,
            ALU.mult), waits=[("ch_act", sq)], chain=True)
        r.op("vector", lambda v: v.tensor_tensor(
            sm[:, cb + 4:cb + 5], sm[:, cb + 2:cb + 3], sm[:, cb + 2:cb + 3],
            ALU.mult), chain=True)
        r.op("vector", lambda v: v.tensor_sub(
            sm[:, cb + 5:cb + 6], sm[:, cb + 3:cb + 4],
            sm[:, cb + 4:cb + 5]), chain=True)
        vd = r.val("ch_dve")
        r.op("scalar", lambda s: s.activation(
            sm[:, cb + 6:cb + 7], sm[:, cb + 5:cb + 6], AF.Sqrt,
            bias=sm[:, 23:24]), waits=[("ch_dve", vd)], chain=True)
        sq2 = r.val("ch_act")
        r.op("vector", lambda v: v.reciprocal(
            sm[:, cb + 7:cb + 8], sm[:, cb + 6:cb + 7]),
            waits=[("ch_act", sq2)], chain=True)
        r.op("vector", lambda v: v.tensor_scalar(
            outap(), xap(), sm[:, cb + 2:cb + 3], sm[:, cb + 7:cb + 8],
            ALU.subtract, ALU.mult), chain=True)
        return r.val("ch_dve")

    y_done = {}
    z_done = {}
    n_done = {}
    tpos_done = {}
    e_done = {}
    v_done = {}
    ve_done = {}
    k_done = {}
    ke_done = {}
    vstage_dma = {}
    kstage_dma = {}
    xld_w = {}
    vinit = r.val("ch_dve")
    NT0 = NW * 4

    def emit_xdma(w):
        bi = w % 2
        lw = [("ch_dve", n_done[4 * (w - 2) + 3])] if w >= 2 else []
        r.op("sync", lambda sy, w=w, bi=bi: sy.dma_start(
            out=x_buf[bi][:],
            in_=x_in[w * 512:(w + 1) * 512, :]
            .rearrange("(t p) d -> p t d", p=128)),
            waits=lw, incs=[(f"d_x{bi}", 16)])
        xld_w[w] = r.val(f"d_x{bi}")

    emit_xdma(0)
    emit_xdma(1)

    for gt in range(NT0 + 4):
        w, t = divmod(gt, 4)
        if gt < NT0:
            bi = w % 2
            ti = gt % 2
            cb = ti * 8
            xap = lambda bi=bi, t=t: x_buf[bi][:, t, :]
            xw = [(f"d_x{bi}", xld_w[w])]
            if gt >= 2:
                xw.append(("ch_dve", y_done[gt - 2]))
            r.op("scalar", lambda sc, xap=xap, cb=cb, ti=ti: sc.activation(
                xsq2[ti][:], xap(), AF.Copy,
                accum_out=sm[:, cb:cb + 1]), waits=xw, chain=True)
            r.op("scalar", lambda sc, xap=xap, cb=cb, ti=ti: sc.activation(
                xsq2[ti][:], xap(), AF.Square,
                accum_out=sm[:, cb + 1:cb + 2]), chain=True)
            a_done = r.val("ch_act")

        g1 = gt - 1
        if 0 <= g1 < NT0:
            w1, t1 = divmod(g1, 4)
            bi1 = w1 % 2
            ti1 = g1 % 2
            tb1 = ti1 * 1024
            ptw = [("ch_dve", n_done[g1]), ("ch_pool", consts_done)]
            if g1 >= 2:
                ptw.append(("ch_dve", e_done[g1 - 2]))
            r.wait("tensor", ptw)
            for c in range(NCH):
                r.op("tensor", lambda te, ti1=ti1, c=c, tb1=tb1: te.transpose(
                    ps_t[:, tb1 + c * 128:tb1 + (c + 1) * 128],
                    xn_tok[ti1][:, c * 128:(c + 1) * 128], ident16[:]))
            tpos_done[g1] = r.milestone("tensor")
            r.op("vector", lambda v, bi1=bi1, t1=t1, tb1=tb1: v.tensor_copy(
                xn_win[bi1][:, :, t1 * 128:(t1 + 1) * 128],
                ps_t[:, tb1:tb1 + 1024].rearrange("p (c n) -> p c n",
                                                  c=NCH)),
                waits=[("ch_pe", tpos_done[g1])], chain=True)
            e_done[g1] = r.val("ch_dve")

        g2 = gt - 2
        if 0 <= g2 < NT0:
            w2, t2 = divmod(g2, 4)
            bi2 = w2 % 2
            vh2 = g2 % 2
            vw_ = [("ch_dve", e_done[g2])]
            if g2 < 2:
                vw_.append(("d_w", w0_loads))
            else:
                vw_.append(("ch_act", ve_done[g2 - 2]))
            r.wait("tensor", vw_)
            for half in range(2):
                for c in range(NCH):
                    r.op("tensor", lambda te, bi2=bi2, t2=t2, c=c, half=half,
                         vh2=vh2: te.matmul(
                             ps_v[:, vh2 * 1024 + half * 512:
                                  vh2 * 1024 + (half + 1) * 512],
                             xn_win[bi2][:, c, t2 * 128:(t2 + 1) * 128],
                             wv_sb[:, c, half * 512:(half + 1) * 512],
                             start=(c == 0), stop=(c == NCH - 1)))
            v_done[g2] = r.milestone("tensor")

        if gt < NT0:
            r.op("vector", lambda v, cb=cb: v.tensor_scalar(
                sm[:, cb + 2:cb + 3], sm[:, cb:cb + 1], 1.0 / D, None,
                ALU.mult),
                waits=[("ch_act", a_done)], chain=True)
            r.op("vector", lambda v, cb=cb: v.tensor_tensor(
                sm[:, cb + 4:cb + 5], sm[:, cb + 2:cb + 3],
                sm[:, cb + 2:cb + 3], ALU.mult), chain=True)
            r.op("vector", lambda v, cb=cb: v.tensor_scalar(
                sm[:, cb + 5:cb + 6], sm[:, cb + 1:cb + 2], 1.0 / D,
                sm[:, cb + 4:cb + 5], ALU.mult, ALU.subtract), chain=True)
            y_done[gt] = r.val("ch_dve")
            r.op("scalar", lambda sc, cb=cb: sc.activation(
                sm[:, cb + 6:cb + 7], sm[:, cb + 5:cb + 6], AF.Sqrt,
                bias=sm[:, 23:24]),
                waits=[("ch_dve", y_done[gt])], chain=True)
            z_done[gt] = r.val("ch_act")
            r.op("vector", lambda v, cb=cb: v.reciprocal(
                sm[:, cb + 7:cb + 8], sm[:, cb + 6:cb + 7]),
                waits=[("ch_act", z_done[gt])], chain=True)
            nw_ = [("ch_pe", tpos_done[gt - 2])] if gt >= 2 else []
            r.op("vector", lambda v, xap=xap, ti=ti, cb=cb: v.tensor_scalar(
                xn_tok[ti][:], xap(), sm[:, cb + 2:cb + 3],
                sm[:, cb + 7:cb + 8], ALU.subtract, ALU.mult),
                waits=nw_, chain=True)
            n_done[gt] = r.val("ch_dve")

        wk = w - 1
        if 0 <= wk < NW:
            bik = wk % 2
            ks = wk % 2
            for p in (2 * t, 2 * t + 1):
                kh = p % 2
                kw_ = [("ch_dve", e_done[4 * wk + 3])]
                if wk == 0 and p < 2:
                    kw_.append(("d_w", w0_loads))
                if p >= 2:
                    kw_.append(("ch_dve", ke_done[(wk, p - 2)]))
                r.wait("tensor", kw_)
                for c in range(NCH):
                    r.op("tensor", lambda te, bik=bik, p=p, c=c, kh=kh:
                         te.matmul(
                             ps_k[:, kh * 512:(kh + 1) * 512],
                             wk_sb[:, p * NCH + c, :], xn_win[bik][:, c, :],
                             start=(c == 0), stop=(c == NCH - 1)))
                k_done[(wk, p)] = r.milestone("tensor")
                ekw = [("ch_pe", k_done[(wk, p)])]
                if p == 0 and ks in kstage_dma:
                    ekw.append((f"d_sk{ks}", kstage_dma[ks]))
                r.op("vector", lambda v, p=p, ks=ks, kh=kh: v.tensor_scalar(
                    k_stage[ks][:, p, :], ps_k[:, kh * 512:(kh + 1) * 512],
                    bkp[:, p:p + 1], None, ALU.add), waits=ekw, chain=True)
                ke_done[(wk, p)] = r.val("ch_dve")
            if t == 3:
                r.op("scalar", lambda sc, wk=wk, ks=ks: sc.dma_start(
                    out=scr_k[:, :, wk * 512:(wk + 1) * 512]
                    .rearrange("a p n -> p a n"),
                    in_=k_stage[ks][:]),
                    waits=[("ch_dve", ke_done[(wk, NP - 1)])],
                    incs=[(f"d_sk{ks}", 16)])
                kstage_dma[ks] = r.val(f"d_sk{ks}")

        if 0 <= g2 < NT0:
            evw = [("ch_pe", v_done[g2])]
            if vh2 in vstage_dma:
                evw.append((f"d_sv{vh2}", vstage_dma[vh2]))
            if g2 < 2:
                evw.append(("ch_dve", vinit))
            pv4 = ps_v[:, vh2 * 1024:(vh2 + 1) * 1024].rearrange(
                "p (a h d) -> p a h d", a=NP, h=2)
            vst3 = v_stage[vh2][:].rearrange("p (a y) -> p a y", a=NP)
            r.op("scalar", lambda sc, pv4=pv4, vst3=vst3: sc.copy(
                vst3[:, :, 0:64], pv4[:, :, 0, :]), waits=evw, chain=True)
            r.op("scalar", lambda sc, pv4=pv4, vst3=vst3: sc.copy(
                vst3[:, :, 65:129], pv4[:, :, 1, :]), chain=True)
            ve_done[g2] = r.val("ch_act")
            r.op("scalar", lambda sc, g2=g2, vh2=vh2: sc.dma_start(
                out=scr_v[g2, :, :], in_=v_stage[vh2][:]),
                waits=[("ch_act", ve_done[g2])], incs=[(f"d_sv{vh2}", 16)])
            vstage_dma[vh2] = r.val(f"d_sv{vh2}")

        if t == 3 and gt < NT0 and w + 2 < NW:
            emit_xdma(w + 2)

    own_tpos = {}
    ownld = {}
    xnt_dma = {}
    p0_tail_dve = r.val("ch_dve")
    p0_tail_pe = r.val("ch_pe")
    for qt in range(OWN_QT):
        bi = qt % 2
        lw = ([("ch_dve", ownld[qt - 2])] if qt >= 2
              else [("ch_dve", p0_tail_dve)])
        r.op("sync", lambda sy, qt=qt, bi=bi: sy.dma_start(
            out=x_buf[bi][:, 0, :], in_=xo_in[qt * 128:(qt + 1) * 128, :]),
            waits=lw, incs=[(f"d_x{bi}", 16)])
        xld = r.val(f"d_x{bi}")
        ti = qt % 2
        tw = [(f"d_x{bi}", xld)]
        tw.append(("ch_pe", own_tpos[qt - 2] if qt >= 2 else p0_tail_pe))
        if F_SPARSE and qt >= 2:
            tw.append((f"d_xn{ti}", xnt_dma[ti]))
        ln_tile(lambda bi=bi: x_buf[bi][:, 0, :],
                lambda ti=ti: xn_tok[ti][:], tw, cb=(qt % 2) * 8)
        xn_done = r.val("ch_dve")
        if F_SPARSE:
            r.op("scalar", lambda sc, qt=qt, ti=ti: sc.dma_start(
                out=scr_xnt[qt * 128:(qt + 1) * 128, :], in_=xn_tok[ti][:]),
                waits=[("ch_dve", xn_done)], incs=[(f"d_xn{ti}", 16)])
            xnt_dma[ti] = r.val(f"d_xn{ti}")
        tb = (qt % 2) * 1024
        r.wait("tensor", [("ch_dve", xn_done)])
        for c in range(NCH):
            r.op("tensor", lambda te, ti=ti, c=c, tb=tb: te.transpose(
                ps_t[:, tb + c * 128:tb + (c + 1) * 128],
                xn_tok[ti][:, c * 128:(c + 1) * 128], ident16[:]))
        own_tpos[qt] = r.milestone("tensor")
        r.op("vector", lambda v, qt=qt, tb=tb: v.tensor_copy(
            xn_own[:, :, qt * 128:(qt + 1) * 128],
            ps_t[:, tb:tb + 1024].rearrange("p (c n) -> p c n", c=NCH)),
            waits=[("ch_pe", own_tpos[qt])], chain=True)
        ownld[qt] = r.val("ch_dve")

    xn_own_done = r.val("ch_dve")
    r.wait("tensor", [("ch_dve", xn_own_done)])

    for p in range(NP):
        for qb in range(QB):
            ph = (p * QB + qb) % 2
            if p * QB + qb >= 2:
                r.wait("tensor", [("ch_dve", qevac[p * QB + qb - 2])])
            for c in range(NCH):
                r.op("tensor", lambda te, p=p, qb=qb, c=c, ph=ph: te.matmul(
                    ps_k[:, ph * 512:(ph + 1) * 512],
                    wq_sb[:, p * NCH + c, :],
                    xn_own[:, c, qb * 512:(qb + 1) * 512],
                    start=(c == 0), stop=(c == NCH - 1)))
            qmm = r.milestone("tensor")
            r.op("vector", lambda v, p=p, qb=qb, ph=ph: v.tensor_scalar(
                q_sb[:, p, qb * 512:(qb + 1) * 512],
                ps_k[:, ph * 512:(ph + 1) * 512],
                bqp[:, p:p + 1], None, ALU.add),
                waits=[("ch_pe", qmm)], chain=True)
            if p == 0 and qb == 0:
                qevac = {}
            qevac[p * QB + qb] = r.val("ch_dve")

    for tt in range(OWN_QT):
        gh = tt % 2
        if tt >= 2:
            r.wait("tensor", [("ch_dve", gevac[tt - 2])])
        for c in range(NCH):
            r.op("tensor", lambda te, tt=tt, c=c, gh=gh: te.matmul(
                ps_v[:, gh * 1024:gh * 1024 + NE],
                xn_own[:, c, tt * 128:(tt + 1) * 128],
                wg_sb[:, c, :], start=(c == 0), stop=(c == NCH - 1)))
        gmm = r.milestone("tensor")
        r.op("vector", lambda v, tt=tt, gh=gh: v.tensor_copy(
            scores[:, tt * NE:(tt + 1) * NE],
            ps_v[:, gh * 1024:gh * 1024 + NE]),
            waits=[("ch_pe", gmm)], chain=True)
        if tt == 0:
            gevac = {}
        gevac[tt] = r.val("ch_dve")

    r.wait("sync", [(x, r.val(x)) for x in
                    ("d_x0", "d_x1", "d_sv0", "d_sv1", "d_sk0", "d_sk1",
                     "d_w", "d_xn0", "d_xn1")])
    blocks = [("P0", dict(r.ops))]
    r.ops = {e: [] for e in Rec.ENGS}

    close("x_buf0", "x_buf1", "xn_tok0", "xn_tok1", "xn_win0", "xn_win1",
          "v_stage0", "v_stage1", "k_stage0", "k_stage1", "xsq2_0", "xsq2_1",
          "wq_sb", "wk_sb", "wv_sb", "wg_sb", "ps_t", "ps_v", "ps_k")
    if F_SPARSE:
        close("xn_own")

    o_feat = sb("o_feat", [128, NP, OWNQ], BF16)
    p_buf = sb("p_buf", [128, 4, 2, 512], BF16)
    pm_all = sb("pm_all", [128, NM * 12 * 128], BF16)
    pm0_sb = sb("pm0_sb", [128, Q0KT * 128], F32)
    o_tmp = sb("o_tmp", [128, 512], F32)
    otok2 = sb("otok2", [128, 128], F32)
    p0f = sb("p0f", [128, Q0KT * 128], F32)
    p0t = sb("p0t", [128, Q0KT * 128], BF16)
    if F_SPARSE:
        combine = sb("combine", [128, OWN_QT * NE], F32)
        gm = sb("gm", [128, 4 * NE], F32)
        gmsk1 = sb("gmsk1", [128, OWN_QT * NE], F32)
        gmsk2 = sb("gmsk2", [128, OWN_QT * NE], F32)
        msk_sb = sb("msk_sb", [128, OWN_QT * NE], F32)
        er_sb = sb("er_sb", [128, OWN_QT * 2], F32)
        triL = sb("triL", [128, 128], F32)
        iota8 = sb("iota8", [128, NE], F32)
        iotac = sb("iotac", [128, 1], F32)
        onec32 = sb("onec32", [128, 1], F32)
        oner32 = sb("oner32", [1, 128], F32)
        cnt_row = sb("cnt_row", [1, OWN_QT * NE], F32)
        bases = sb("bases", [1, OWN_QT * NE], F32)
        pg_sb = sb("pg_sb", [128, NE], F32)
        icol = sb("icol", [128, 2, 8], mybir.dt.int32)
        itok = sb("itok", [128, 2 * OWN_QT], mybir.dt.int32)
        ifill = sb("ifill", [128, NE * CAP // 128], mybir.dt.int32)
        ifill2 = sb("ifill2", [128, NE * CAP // 128], mybir.dt.int32)
        zge = sb("zge", [128, D], F32)
    s_stage = [sb(f"s_stage{i}", [128, D], BF16) for i in range(2)]
    v_sb = [sb(f"v_sb{i}", [128, NT, 130], BF16) for i in range(2)]
    k_sb = [sb(f"k_sb{i}", [128, seq], BF16) for i in range(2)]

    PS = psum("PS", [128, 2048], F32)
    PO = [psum(f"PO{i}", [65, 256], F32) for i in range(2)]
    PSmp = psum("PSmp", [128, 512], F32)
    PSdn = psum("PSdn", [128, 512], F32)
    psdn_fence = [0]

    r.op("sync", lambda sy: sy.dma_start(
        out=pm_all[:], in_=pm_in), incs=[("d_pm", 16)])
    r.op("sync", lambda sy: sy.dma_start(out=pm0_sb[:], in_=pm0_in),
         incs=[("d_pm", 16)])
    pm_ld = r.val("d_pm")

    def pmv(mm, slot):
        base = (mm * 12 + slot) * 128
        return pm_all[:, base:base + 128]

    kv_ld = {}
    kv_A = {}
    sv_finals = [r.val("d_sv0"), r.val("d_sv1")]

    def load_pair(p, waits):
        vwaits = list(waits)
        if p < 2:
            vwaits += [("d_sv0", sv_finals[0]), ("d_sv1", sv_finals[1])]
            r.op("sync", lambda sy, p=p: sy.dma_start(
                out=k_sb[p % 2][:, 0:2048], in_=scr_k[p, :, 0:2048]),
                waits=waits, incs=[(f"d_k{p % 2}", 16)])
            kA = r.val(f"d_k{p % 2}")
            r.op("sync", lambda sy, p=p: sy.dma_start(
                out=k_sb[p % 2][:, 2048:seq], in_=scr_k[p, :, 2048:seq]),
                incs=[(f"d_k{p % 2}", 16)])
            r.op("scalar", lambda v, p=p: v.dma_start(
                out=v_sb[p % 2][:, 0:16, :],
                in_=scr_v[0:16, :, p * 130:(p + 1) * 130]
                .rearrange("k p y -> p k y")),
                waits=vwaits, incs=[(f"d_v{p % 2}", 16)])
            vA = r.val(f"d_v{p % 2}")
            r.op("scalar", lambda v, p=p: v.dma_start(
                out=v_sb[p % 2][:, 16:NT, :],
                in_=scr_v[16:NT, :, p * 130:(p + 1) * 130]
                .rearrange("k p y -> p k y")),
                incs=[(f"d_v{p % 2}", 16)])
            kv_A[p] = (kA, vA)
        else:
            r.op("sync", lambda sy, p=p: sy.dma_start(
                out=k_sb[p % 2][:], in_=scr_k[p, :, :]),
                waits=waits, incs=[(f"d_k{p % 2}", 16)])
            r.op("scalar", lambda v, p=p: v.dma_start(
                out=v_sb[p % 2][:],
                in_=scr_v[:, :, p * 130:(p + 1) * 130]
                .rearrange("k p y -> p k y")),
                waits=vwaits, incs=[(f"d_v{p % 2}", 16)])
        kv_ld[p] = (r.val(f"d_k{p % 2}"), r.val(f"d_v{p % 2}"))

    load_pair(0, [])
    load_pair(1, [])
    if F_SPARSE:
        r.op("gpsimd", lambda g: g.memset(triL[:], 1.0), chain=True)
        r.op("gpsimd", lambda g: g.affine_select(
            triL[:], triL[:], [[1, 128]], ALU.is_ge, 0.0,
            channel_multiplier=-1, base=-1), chain=True)
        r.op("gpsimd", lambda g: g.iota(
            iota8[:], [[1, NE]], channel_multiplier=0,
            allow_small_or_imprecise_dtypes=True), chain=True)
        r.op("gpsimd", lambda g: g.iota(
            iotac[:], [[1, 1]], channel_multiplier=1,
            allow_small_or_imprecise_dtypes=True), chain=True)
        r.op("gpsimd", lambda g: g.iota(
            itok[:, 0:OWN_QT], [[128, OWN_QT]], channel_multiplier=1),
            chain=True)
        r.op("gpsimd", lambda g: g.tensor_scalar_add(
            itok[:, OWN_QT:2 * OWN_QT], itok[:, 0:OWN_QT], OWNQ),
            chain=True)
        r.op("gpsimd", lambda g: g.memset(ifill[:], 0), chain=True)
        r.op("gpsimd", lambda g: g.memset(ifill2[:], 2 * OWNQ),
             chain=True)
        gp_consts = r.val("ch_pool")
        r.op("vector", lambda v: v.memset(onec32[:], 1.0), chain=True)
        r.op("vector", lambda v: v.memset(oner32[:], 1.0), chain=True)
        r.op("vector", lambda v: v.memset(zge[:], 0.0), chain=True)
        r.op("vector", lambda v: v.memset(bases[0:1, 0:NE], 0.0), chain=True)
        r.op("gpsimd", lambda g: g.dma_start(
            out=scr_gsrc[0:NSLOT, :].rearrange("(t p) o -> p (t o)", p=128),
            in_=ifill[:], allow_slow_non_contiguous=True),
            waits=[("ch_pool", gp_consts)], incs=[("d_ix", 16)])
        r.op("gpsimd", lambda g: g.dma_start(
            out=scr_gdst[0:NSLOT, :].rearrange("(t p) o -> p (t o)", p=128),
            in_=ifill2[:], allow_slow_non_contiguous=True),
            incs=[("d_ix", 16)])
        prefill_ix = r.val("d_ix")
        cast_gate = [("d_k0", kv_ld[0][0]), ("d_v0", kv_ld[0][1]),
                     ("d_k1", kv_ld[1][0]), ("d_v1", kv_ld[1][1]),
                     ("d_pm", pm_ld)]
        for e in range(NE):
            for dst, src in ((scr_w1, w1_in), (scr_w3, w3_in),
                             (scr_w2, w2_in)):
                r.op("gpsimd", lambda g, dst=dst, src=src, e=e: g.dma_start(
                    out=dst[e], in_=src[e]),
                    waits=cast_gate if (e == 0 and dst is scr_w1) else (),
                    incs=[("d_wc", 16)])
        wc_done = r.val("d_wc")
        zv = r.val("ch_dve")
        for zt in range(2 * OWN_QT):
            r.op("sync", lambda sc, zt=zt: sc.dma_start(
                out=scr_ge[zt * 128:(zt + 1) * 128, :], in_=zge[:]),
                waits=[("ch_dve", zv)] if zt == 0 else (),
                incs=[("d_ge", 16)])
        ge_zeroed = r.val("d_ge")

        gv = {}
        gsteps = []
        pg_evac = {}
        ix_after = {}

        def g1a(tt):
            sc_ap = scores[:, tt * NE:(tt + 1) * NE]
            r.op("vector", lambda v, sc=sc_ap: v.reduce_max(
                sm[:, 0:1], sc, axis=AX.X),
                waits=[("ch_act", r.val("ch_act")),
                       ("ch_pool", gp_consts)], chain=True)
            r.op("vector", lambda v, sc=sc_ap, tt=tt: v.tensor_scalar(
                gmsk1[:, tt * NE:(tt + 1) * NE], sc, sm[:, 0:1], None,
                ALU.is_equal), chain=True)
            r.op("vector", lambda v, sc=sc_ap, tt=tt: v.scalar_tensor_tensor(
                gm[:, NE:2 * NE], gmsk1[:, tt * NE:(tt + 1) * NE], -1e9, sc,
                ALU.mult, ALU.add), chain=True)
            r.op("vector", lambda v: v.reduce_max(
                sm[:, 1:2], gm[:, NE:2 * NE], axis=AX.X), chain=True)
            r.op("vector", lambda v, tt=tt: v.tensor_scalar(
                gmsk2[:, tt * NE:(tt + 1) * NE], gm[:, NE:2 * NE],
                sm[:, 1:2], None, ALU.is_equal), chain=True)
            r.op("vector", lambda v: v.tensor_sub(
                sm[:, 2:3], sm[:, 0:1], sm[:, 1:2]), chain=True)
            dcol = r.val("ch_dve")
            r.op("scalar", lambda scl: scl.activation(
                sm[:, 3:4], sm[:, 2:3], AF.Sigmoid),
                waits=[("ch_dve", dcol)], chain=True)
            r.op("scalar", lambda scl: scl.activation(
                sm[:, 4:5], sm[:, 2:3], AF.Sigmoid, scale=-1.0), chain=True)
            gv[("sig", tt)] = r.val("ch_act")

        def g1b(tt):
            r.op("vector", lambda v, tt=tt: v.tensor_scalar(
                gm[:, 3 * NE:4 * NE], gmsk1[:, tt * NE:(tt + 1) * NE],
                sm[:, 3:4], None, ALU.mult),
                waits=[("ch_act", gv[("sig", tt)])], chain=True)
            r.op("vector", lambda v, tt=tt: v.scalar_tensor_tensor(
                combine[:, tt * NE:(tt + 1) * NE],
                gmsk2[:, tt * NE:(tt + 1) * NE],
                sm[:, 4:5], gm[:, 3 * NE:4 * NE], ALU.mult, ALU.add),
                chain=True)
            r.op("vector", lambda v, tt=tt: v.tensor_add(
                msk_sb[:, tt * NE:(tt + 1) * NE],
                gmsk1[:, tt * NE:(tt + 1) * NE],
                gmsk2[:, tt * NE:(tt + 1) * NE]), chain=True)
            r.op("vector", lambda v, tt=tt: v.tensor_mul(
                gm[:, NE:2 * NE], gmsk1[:, tt * NE:(tt + 1) * NE],
                iota8[:]), chain=True)
            r.op("vector", lambda v, tt=tt: v.reduce_sum(
                er_sb[:, tt * 2:tt * 2 + 1], gm[:, NE:2 * NE], axis=AX.X),
                chain=True)
            r.op("vector", lambda v, tt=tt: v.tensor_mul(
                gm[:, NE:2 * NE], gmsk2[:, tt * NE:(tt + 1) * NE],
                iota8[:]), chain=True)
            r.op("vector", lambda v, tt=tt: v.reduce_sum(
                er_sb[:, tt * 2 + 1:tt * 2 + 2], gm[:, NE:2 * NE],
                axis=AX.X), chain=True)

        def gcmb():
            r.op("scalar", lambda sc: sc.dma_start(
                out=scr_cmb.rearrange("(t p) c -> p t c", p=128),
                in_=combine[:].rearrange("p (t c) -> p t c", c=NE)),
                waits=[("ch_dve", r.val("ch_dve"))], incs=[("d_cm", 16)])

        def gcnt():
            r.wait("tensor", [("ch_dve", max(r.val("ch_dve"),
                                             psdn_fence[0])),
                              ("ch_pool", gp_consts)])
            for tt in range(OWN_QT):
                r.op("tensor", lambda te, tt=tt: te.matmul(
                    PSdn[0:1, 384 + tt * NE:384 + (tt + 1) * NE],
                    onec32[:], msk_sb[:, tt * NE:(tt + 1) * NE],
                    start=True, stop=True))
            cnt_mm = r.milestone("tensor")
            r.op("vector", lambda v: v.tensor_copy(
                cnt_row[0:1, :], PSdn[0:1, 384:384 + OWN_QT * NE]),
                waits=[("ch_pe", cnt_mm)], chain=True)
            psdn_fence[0] = r.val("ch_dve")

        def gbase():
            for tt in range(1, OWN_QT):
                r.op("vector", lambda v, tt=tt: v.tensor_add(
                    bases[0:1, tt * NE:(tt + 1) * NE],
                    bases[0:1, (tt - 1) * NE:tt * NE],
                    cnt_row[0:1, (tt - 1) * NE:tt * NE]), chain=True)
            gv["bases"] = r.val("ch_dve")

        def g3a(tt):
            quad = tt % 2
            qb = 384 + quad * NE
            pw = pg_evac[tt - 2] if tt >= 2 else gv["bases"]
            r.wait("tensor", [("ch_dve", max(pw, psdn_fence[0]))])
            r.op("tensor", lambda te, tt=tt, qb=qb: te.matmul(
                PSdn[:, qb:qb + NE],
                triL[:], msk_sb[:, tt * NE:(tt + 1) * NE],
                start=True, stop=False))
            r.op("tensor", lambda te, tt=tt, qb=qb: te.matmul(
                PSdn[:, qb:qb + NE],
                oner32[:], bases[0:1, tt * NE:(tt + 1) * NE],
                start=False, stop=True))
            pmm = r.milestone("tensor")
            r.op("vector", lambda v, qb=qb: v.tensor_copy(
                pg_sb[:], PSdn[:, qb:qb + NE]),
                waits=[("ch_pe", pmm)], chain=True)
            pg_evac[tt] = r.val("ch_dve")
            psdn_fence[0] = pg_evac[tt]

        def g3b(tt):
            qq = tt % 2
            iw = ([(f"d_ix{i}", ix_after[tt - 2][i]) for i in range(4)]
                  if tt >= 2 else [])
            first = [True]
            for rk in range(2):
                gmk = gmsk1 if rk == 0 else gmsk2
                r.op("vector", lambda v, tt=tt, gmk=gmk: v.tensor_mul(
                    gm[:, 0:NE], pg_sb[:], gmk[:, tt * NE:(tt + 1) * NE]),
                    waits=iw if first[0] else (), chain=True)
                first[0] = False
                r.op("vector", lambda v: v.reduce_sum(
                    sm[:, 2:3], gm[:, 0:NE], axis=AX.X), chain=True)
                r.op("vector", lambda v, tt=tt, rk=rk: v.scalar_tensor_tensor(
                    sm[:, 3:4], er_sb[:, tt * 2 + rk:tt * 2 + rk + 1],
                    float(CAP), sm[:, 2:3], ALU.mult, ALU.add), chain=True)
                r.op("vector", lambda v: v.tensor_scalar(
                    sm[:, 4:5], sm[:, 2:3], float(CAP), None, ALU.is_ge),
                    chain=True)
                r.op("vector", lambda v: v.scalar_tensor_tensor(
                    sm[:, 5:6], sm[:, 4:5], float(NSLOT + 8192), sm[:, 3:4],
                    ALU.mult, ALU.add), chain=True)
                r.op("vector", lambda v, rk=rk, qq=qq: v.tensor_copy(
                    icol[:, qq, rk:rk + 1], sm[:, 5:6]), chain=True)
            pgd = r.val("ch_dve")
            for rk in range(2):
                sA, sB = f"d_ix{2 * rk}", f"d_ix{2 * rk + 1}"
                wA = [("ch_dve", pgd), (sA, r.val(sA))]
                wB = [("ch_dve", pgd), (sB, r.val(sB))]
                if tt == 0:
                    wA.append(("d_ix", prefill_ix))
                    wB.append(("d_ix", prefill_ix))
                r.op("gpsimd", lambda g, rk=rk, qq=qq, tt=tt:
                     g.indirect_dma_start(
                         out=scr_gsrc,
                         out_offset=bass.IndirectOffsetOnAxis(
                             icol[:, qq, rk:rk + 1], 0),
                         in_=itok[:, tt:tt + 1], in_offset=None),
                     waits=wA, incs=[(sA, 16)])
                r.op("gpsimd", lambda g, rk=rk, qq=qq, tt=tt:
                     g.indirect_dma_start(
                         out=scr_gdst,
                         out_offset=bass.IndirectOffsetOnAxis(
                             icol[:, qq, rk:rk + 1], 0),
                         in_=itok[:, rk * OWN_QT + tt:rk * OWN_QT + tt + 1],
                         in_offset=None),
                     waits=wB, incs=[(sB, 16)])
            ix_after[tt] = tuple(r.val(f"d_ix{i}") for i in range(4))

        def grdbk():
            ix_done = [(f"d_ix{i}", r.val(f"d_ix{i}")) for i in range(4)]
            r.op("sync", lambda sy: sy.dma_start(
                out=gsrc_sb[:],
                in_=scr_gsrc[0:NE * CAP, :]
                .rearrange("(t p) o -> p (t o)", p=128),
                allow_slow_non_contiguous=True),
                waits=ix_done, incs=[("d_gi", 16)])
            r.op("sync", lambda sy: sy.dma_start(
                out=gdst_sb[:],
                in_=scr_gdst[0:NE * CAP, :]
                .rearrange("(t p) o -> p (t o)", p=128),
                allow_slow_non_contiguous=True), incs=[("d_gi", 16)])

        for tt in range(OWN_QT):
            gsteps.append(lambda tt=tt: g1a(tt))
            gsteps.append(lambda tt=tt: g1b(tt))
        gsteps.append(gcmb)
        gsteps.append(gcnt)
        gsteps.append(gbase)
        for tt in range(OWN_QT):
            gsteps.append(lambda tt=tt: g3a(tt))
            gsteps.append(lambda tt=tt: g3b(tt))
        gsteps.append(grdbk)
    else:
        gsteps = []
    gq = [0]

    pair_pe_done = {}
    bias_ready = {}
    exp_done = {}
    pv_done = {}
    Gctr = [0]
    pending = []
    pvq = []
    po_free = [0]
    bias_prev_dve = [0]

    def sample_steps(p, m):
        nkt = 8 * m + 8
        bias_c = 16 + 2 * (m % 2)
        kb = k_sb[p % 2]
        sv = {}

        def st0():
            w0 = [("ch_dve", sv.get("prev_bias", bias_prev_dve[0]))]
            if p < 2 and m >= 2:
                w0.append((f"d_k{p % 2}", kv_ld[p][0]))
            r.wait("tensor", w0)
            for hh in range(2):
                h0 = hh * 64
                kslice = kb[h0:h0 + 64, 0:nkt * 128].rearrange(
                    "p (n s) -> p s n", s=nkt)[:, 0, :]
                r.op("tensor", lambda te, kslice=kslice, h0=h0, p=p, m=m,
                     hh=hh: te.matmul(
                         PSmp[:, hh * 256:hh * 256 + 256], kslice,
                         q_sb[h0:h0 + 64, p, m * 256:(m + 1) * 256],
                         start=True, stop=True, tile_position=(h0, 0)))
                sv["mm%d" % hh] = r.milestone("tensor")
                if hh == 0:
                    r.wait("tensor", [("ch_pe", sv["mm0"])])

        def st1():
            for hh in range(2):
                r.op("vector", lambda v, hh=hh: v.reduce_max(
                    sm[:, 8 + hh:9 + hh], PSmp[:, hh * 256:hh * 256 + 256],
                    axis=AX.X),
                    waits=[("ch_pe", sv["mm1"])] if hh == 0 else (),
                    chain=True)
            sv["red"] = r.val("ch_dve")

        def st2():
            r.wait("tensor", [("ch_dve", sv["red"])])
            for hh in range(2):
                r.op("tensor", lambda te, hh=hh: te.transpose(
                    PSmp[0:1, hh * 256:hh * 256 + 128],
                    sm[:, 8 + hh:9 + hh], ident32[:]))
            sv["tr"] = r.milestone("tensor")

        def st3():
            for hh in range(2):
                r.op("vector", lambda v, hh=hh: v.reduce_max(
                    sm[0:1, 12 + hh:13 + hh],
                    PSmp[0:1, hh * 256:hh * 256 + 128], axis=AX.X),
                    waits=[("ch_pe", sv["tr"])] if hh == 0 else (),
                    chain=True)
            r.op("vector", lambda v: v.tensor_max(
                sm[0:1, 14:15], sm[0:1, 12:13], sm[0:1, 13:14]), chain=True)
            r.op("vector", lambda v: v.tensor_copy(
                sm16[0:1, 0:1], sm[0:1, 14:15]), chain=True)
            sv["cast"] = r.val("ch_dve")

        def st4():
            r.wait("tensor", [("ch_dve", sv["cast"])])
            r.op("tensor", lambda te: te.matmul(
                PSmp[:, 384:385], ones16[0:1, :], sm16[0:1, 0:1],
                start=True, stop=True))
            sv["bc"] = r.milestone("tensor")

        def st5():
            r.op("vector", lambda v: v.tensor_scalar(
                sm[:, bias_c:bias_c + 1], PSmp[:, 384:385],
                -SCALE, None, ALU.mult),
                waits=[("ch_pe", sv["bc"])], chain=True)
            bias_ready[(p, m)] = r.val("ch_dve")
            bias_prev_dve[0] = bias_ready[(p, m)]

        return [st0, st1, st2, st3, st4, st5]

    def dance_steps(p, m, po_evac_val, lastG=None):
        sv = {"ev": po_evac_val}

        def d0():
            while lastG not in pv_done and pvq:
                pvq.pop(0)()
            for hh in range(2):
                r.op("vector", lambda v, hh=hh: v.tensor_copy(
                    o_tmp[0:65, hh * 256:(hh + 1) * 256], PO[hh][:]),
                    waits=[("ch_pe", pv_done[lastG])] if hh == 0 else (),
                    chain=True)
            po_free[0] = r.val("ch_dve")
            sv["ev"] = po_free[0]

        def d1():
            r.wait("tensor", [("ch_dve", max(sv["ev"], psdn_fence[0]))])
            for hh in range(2):
                r.op("tensor", lambda te, hh=hh: te.transpose(
                    PSdn[:, hh * 65:(hh + 1) * 65],
                    o_tmp[0:65, hh * 256:hh * 256 + 128],
                    ident32[0:65, 0:65]))
            sv["t1"] = r.milestone("tensor")

        def d2():
            first = [("ch_pe", sv["t1"])]
            for hh in range(2):
                r.op("vector", lambda v, hh=hh: v.reciprocal(
                    sm[:, 20:21], PSdn[:, hh * 65 + 64:hh * 65 + 65]),
                    waits=first if hh == 0 else (), chain=True)
                r.op("vector", lambda v, hh=hh: v.tensor_scalar(
                    otok2[:, hh * 64:(hh + 1) * 64],
                    PSdn[:, hh * 65:hh * 65 + 64], sm[:, 20:21],
                    None, ALU.mult), chain=True)
            sv["n1"] = r.val("ch_dve")
            psdn_fence[0] = sv["n1"]

        def d3():
            r.wait("tensor", [("ch_dve", max(sv["n1"], psdn_fence[0]))])
            r.op("tensor", lambda te: te.transpose(
                PSdn[:, 256:384], otok2[:], ident32[:]))
            for hh in range(2):
                r.op("tensor", lambda te, hh=hh: te.transpose(
                    PSdn[:, hh * 65:(hh + 1) * 65],
                    o_tmp[0:65, hh * 256 + 128:hh * 256 + 256],
                    ident32[0:65, 0:65]))
            sv["t2"] = r.milestone("tensor")

        def d4():
            r.op("vector", lambda v, p=p, m=m: v.tensor_scalar(
                o_feat[:, p, m * 256:m * 256 + 128],
                PSdn[:, 256:384], bvt[:, p:p + 1], None, ALU.add),
                waits=[("ch_pe", sv["t2"])], chain=True)
            for hh in range(2):
                r.op("vector", lambda v, hh=hh: v.reciprocal(
                    sm[:, 20:21], PSdn[:, hh * 65 + 64:hh * 65 + 65]),
                    chain=True)
                r.op("vector", lambda v, hh=hh: v.tensor_scalar(
                    otok2[:, hh * 64:(hh + 1) * 64],
                    PSdn[:, hh * 65:hh * 65 + 64], sm[:, 20:21],
                    None, ALU.mult), chain=True)
            sv["n2"] = r.val("ch_dve")
            psdn_fence[0] = sv["n2"]

        def d5():
            r.wait("tensor", [("ch_dve", max(sv["n2"], psdn_fence[0]))])
            r.op("tensor", lambda te: te.transpose(
                PSdn[:, 256:384], otok2[:], ident32[:]))
            sv["t3"] = r.milestone("tensor")

        def d6():
            r.op("vector", lambda v, p=p, m=m: v.tensor_scalar(
                o_feat[:, p, m * 256 + 128:m * 256 + 256],
                PSdn[:, 256:384], bvt[:, p:p + 1], None, ALU.add),
                waits=[("ch_pe", sv["t3"])], chain=True)
            psdn_fence[0] = r.val("ch_dve")

        if po_evac_val is None:
            return [d0, d1, d2, d3, d4, d5, d6]
        return [d1, d2, d3, d4, d5, d6]

    for p in range(NP):
        hd = kv_A[p] if p < 2 else kv_ld[p]
        pw = [(f"d_k{p % 2}", hd[0]), (f"d_v{p % 2}", hd[1]),
              ("d_pm", pm_ld)]
        r.wait("tensor", pw)

        for m in range(NM):
            if p < 2 and m == 2:
                r.wait("tensor", [(f"d_k{p % 2}", kv_ld[p][0]),
                                  (f"d_v{p % 2}", kv_ld[p][1])])
            nkt = 8 * m + 8
            ngr = nkt // 2
            bias_c = 16 + 2 * (m % 2)

            if (p, m) not in bias_ready:
                if F_SAMPLE:
                    for st in sample_steps(p, m):
                        st()
                else:
                    r.op("vector", lambda v, bias_c=bias_c: v.memset(
                        sm[:, bias_c:bias_c + 2], -12.0), chain=True)
                    bias_ready[(p, m)] = r.val("ch_dve")

            smp = sample_steps(p, m + 1) if m + 1 < NM else []
            steps = smp[:2] + pending + smp[2:]
            pending = []
            si = 0

            for g in range(ngr):
                G = Gctr[0]
                Gctr[0] += 1
                kts = (2 * g, 2 * g + 1)
                quad0 = (2 * g) % 4
                par = G % 4
                sw = []
                if G - 2 in exp_done:
                    sw.append(("ch_act", exp_done[G - 2]))
                if g == 0:
                    sw.append(("ch_dve", bias_ready[(p, m)]))
                r.wait("tensor", sw)
                for kt in kts:
                    for hh in range(2):
                        h0 = hh * 64
                        r.op("tensor", lambda te, p=p, m=m, kt=kt, hh=hh,
                             h0=h0: te.matmul(
                                 PS[:, hh * 1024 + (kt % 4) * 256:
                                    hh * 1024 + (kt % 4) * 256 + 256],
                                 k_sb[p % 2][h0:h0 + 64,
                                             kt * 128:(kt + 1) * 128],
                                 q_sb[h0:h0 + 64, p, m * 256:(m + 1) * 256],
                                 start=True, stop=True,
                                 tile_position=(h0, 0)))
                smm = r.milestone("tensor")
                ew = [("ch_pe", smm)]
                if G - 4 in pv_done:
                    ew.append(("ch_pe", pv_done[G - 4]))
                ps_in = PS[:].rearrange("p (h n) -> p h n", h=2)[
                    :, :, quad0 * 256:quad0 * 256 + 512]
                r.op("scalar", lambda s, par=par, quad0=quad0, ps_in=ps_in,
                     bias_c=bias_c: s.activation(
                         p_buf[:, par, :, :], ps_in, AF.Exp,
                         bias=sm[:, bias_c:bias_c + 1], scale=SCALE),
                     waits=ew, chain=True)
                exp_done[G] = r.val("ch_act")
                nmask = 0
                for kt in (kts if F_MASK else ()):
                    slots = []
                    i1 = kt - 8 * m
                    if 0 <= i1 < 8:
                        slots.append((0, i1))
                    i2 = kt - (8 * m + 4)
                    if 0 <= i2 < 4:
                        slots.append((1, 8 + i2))
                    for sub, slot in slots:
                        for hh in range(2):
                            col = (kt % 2) * 256 + sub * 128
                            r.op("vector", lambda v, par=par, hh=hh, col=col,
                                 m=m, slot=slot: v.tensor_mul(
                                     p_buf[:, par, hh, col:col + 128],
                                     p_buf[:, par, hh, col:col + 128],
                                     pmv(m, slot)),
                                 waits=[("ch_act", exp_done[G])], chain=True)
                            nmask += 1
                vw = [("ch_act", exp_done[G])]
                if nmask:
                    vw.append(("ch_dve", r.val("ch_dve")))

                lastkt = -1 if (m == 0 and F_EXACT) else nkt - 1

                def emit_pv(vw=vw, kts=kts, par=par, p=p, lastkt=lastkt,
                            G=G, g0=(g == 0)):
                    if g0:
                        vw = vw + [("ch_dve", po_free[0])]
                    r.wait("tensor", vw)
                    for kt in kts:
                        for hh in range(2):
                            r.op("tensor", lambda te, p=p, kt=kt, hh=hh,
                                 par=par: te.matmul(
                                     PO[hh][:],
                                     v_sb[p % 2][:, kt,
                                                 hh * 65:(hh + 1) * 65],
                                     p_buf[:, par, hh,
                                           (kt % 2) * 256:
                                           (kt % 2) * 256 + 256],
                                     start=(kt == 0), stop=(kt == lastkt)))
                    pv_done[G] = r.milestone("tensor")

                pvq.append(emit_pv)
                if len(pvq) > 3:
                    pvq.pop(0)()
                if si < len(steps):
                    steps[si]()
                    si += 1
                elif gq[0] < len(gsteps) and G % 2 == 0:
                    gsteps[gq[0]]()
                    gq[0] += 1
            defer_po = F_DANCE and 0 < m < NM - 1
            if not defer_po:
                while pvq:
                    pvq.pop(0)()
            while si < len(steps):
                steps[si]()
                si += 1

            if m == 0 and F_EXACT:
                r.wait("tensor", [("ch_act", r.val("ch_act"))])
                for hh in range(2):
                    h0 = hh * 64
                    r.op("tensor", lambda te, p=p, hh=hh, h0=h0: te.matmul(
                        PS[:, hh * 1024:hh * 1024 + Q0KT * 128],
                        q_sb[h0:h0 + 64, p, 0:128],
                        k_sb[p % 2][h0:h0 + 64, 0:Q0KT * 128],
                        start=True, stop=True, tile_position=(h0, 0)))
                q0mm = r.milestone("tensor")
                for hh in range(2):
                    r.op("vector", lambda v, hh=hh: v.tensor_add(
                        PS[:, hh * 1024:hh * 1024 + Q0KT * 128],
                        PS[:, hh * 1024:hh * 1024 + Q0KT * 128],
                        pm0_sb[:]),
                        waits=[("ch_pe", q0mm)], chain=True)
                    r.op("vector", lambda v, hh=hh: v.reduce_max(
                        sm[:, 8 + hh:9 + hh],
                        PS[:, hh * 1024:hh * 1024 + Q0KT * 128],
                        axis=AX.X), chain=True)
                    r.op("vector", lambda v, hh=hh: v.tensor_scalar(
                        sm[:, 10 + hh:11 + hh], sm[:, 8 + hh:9 + hh],
                        -SCALE, None, ALU.mult), chain=True)
                q0bias = r.val("ch_dve")
                for hh in range(2):
                    r.op("scalar", lambda s, hh=hh: s.activation(
                        p0f[:], PS[:, hh * 1024:hh * 1024 + Q0KT * 128],
                        AF.Exp, bias=sm[:, 10 + hh:11 + hh], scale=SCALE),
                        waits=[("ch_dve", q0bias),
                               ("ch_pe", r.val("ch_pe"))], chain=True)
                    e0 = r.val("ch_act")
                    r.wait("tensor", [("ch_act", e0)])
                    for kt in range(Q0KT):
                        r.op("tensor", lambda te, kt=kt, hh=hh: te.transpose(
                            PS[:, hh * 1024 + 512 + kt * 128:
                               hh * 1024 + 512 + (kt + 1) * 128],
                            p0f[:, kt * 128:(kt + 1) * 128], ident32[:]))
                    t0 = r.milestone("tensor")
                    r.op("vector", lambda v, hh=hh: v.tensor_copy(
                        p0t[:], PS[:, hh * 1024 + 512:hh * 1024 + 1024]),
                        waits=[("ch_pe", t0)], chain=True)
                    c0 = r.val("ch_dve")
                    r.wait("tensor", [("ch_dve", c0)])
                    for kt in range(Q0KT):
                        r.op("tensor", lambda te, p=p, kt=kt, hh=hh:
                             te.matmul(
                                 PO[hh][:, 0:128],
                                 v_sb[p % 2][:, kt, hh * 65:(hh + 1) * 65],
                                 p0t[:, kt * 128:(kt + 1) * 128],
                                 start=False, stop=(kt == Q0KT - 1)))
                    r.milestone("tensor")

            if defer_po:
                if F_DANCE:
                    pending = pending + dance_steps(p, m, None,
                                                    Gctr[0] - 1)
            else:
                pvd = r.val("ch_pe")
                for hh in range(2):
                    r.op("vector", lambda v, hh=hh: v.tensor_copy(
                        o_tmp[0:65, hh * 256:(hh + 1) * 256], PO[hh][:]),
                        waits=[("ch_pe", pvd)], chain=True)
                po_free[0] = r.val("ch_dve")
                if F_DANCE:
                    pending = pending + dance_steps(p, m, po_free[0])

        pair_pe_done[p] = r.milestone("tensor")
        if p + 2 < NP:
            load_pair(p + 2, [("ch_pe", pair_pe_done[p])])

    close("k_sb0", "k_sb1")
    wo_sb = sb("wo_sb", [128, NCH, D], BF16)
    r.op("gpsimd", lambda g: g.dma_start(
        out=wo_sb[:], in_=wo_in),
        waits=[("ch_pe", pair_pe_done[NP - 1])], incs=[("d_w", 16)])
    wo_ld = r.val("d_w")

    while pvq:
        pvq.pop(0)()
    for st in pending:
        st()
    pending = []
    while gq[0] < len(gsteps):
        gsteps[gq[0]]()
        gq[0] += 1

    sdma = {}
    r.wait("tensor", [("d_w", wo_ld), ("ch_dve", r.val("ch_dve")),
                      ("ch_act", r.val("ch_act"))])
    for qt in range(OWN_QT):
        sh = qt % 2
        for mh in range(2):
            quad = (qt * 2 + mh) % 4
            if qt * 2 + mh >= 4:
                r.wait("tensor", [("ch_dve", sevac[qt * 2 + mh - 4])])
            for p in range(NP):
                r.op("tensor", lambda te, qt=qt, mh=mh, p=p, quad=quad:
                     te.matmul(
                         PS[:, quad * 512:(quad + 1) * 512],
                         o_feat[:, p, qt * 128:(qt + 1) * 128],
                         wo_sb[:, p, mh * 512:(mh + 1) * 512],
                         start=(p == 0), stop=False))
            r.op("tensor", lambda te, mh=mh, quad=quad: te.matmul(
                PS[:, quad * 512:(quad + 1) * 512], ones16[0:1, :],
                bo_sb[0:1, mh * 512:(mh + 1) * 512],
                start=False, stop=True))
            omm = r.milestone("tensor")
            evw = [("ch_pe", omm)]
            if mh == 0 and sh in sdma:
                evw.append((f"d_ss{sh}", sdma[sh]))
            r.op("vector", lambda v, sh=sh, mh=mh, quad=quad: v.tensor_copy(
                s_stage[sh][:, mh * 512:(mh + 1) * 512],
                PS[:, quad * 512:(quad + 1) * 512]), waits=evw, chain=True)
            if qt == 0 and mh == 0:
                sevac = {}
            sevac[qt * 2 + mh] = r.val("ch_dve")
        r.op("scalar", lambda s, qt=qt, sh=sh: s.dma_start(
            out=scr_s[qt, :, :], in_=s_stage[sh][:]),
            waits=[("ch_dve", sevac[qt * 2 + 1])], incs=[(f"d_ss{sh}", 16)])
        sdma[sh] = r.val(f"d_ss{sh}")

    r.wait("sync", [(x, r.val(x)) for x in
                    ("d_k0", "d_k1", "d_v0", "d_v1", "d_pm", "d_ss0",
                     "d_ss1", "d_w", "d_ix", "d_ix0", "d_ix1", "d_ix2",
                     "d_ix3", "d_ge", "d_cm", "d_gi")])
    blocks.append(("P2", dict(r.ops)))
    r.ops = {e: [] for e in Rec.ENGS}

    close("wo_sb", "v_sb0", "v_sb1", "p_buf", "pm_all", "pm0_sb",
          "o_tmp", "otok2", "p0f", "p0t", "s_stage0", "s_stage1",
          "o_feat", "PS", "PO0", "PO1", "PSmp", "PSdn", "q_sb",
          "combine", "gm", "gmsk1", "gmsk2", "msk_sb", "er_sb", "triL",
          "iota8", "iotac", "onec32", "oner32", "cnt_row", "bases",
          "pg_sb", "icol", "itok", "ifill", "ifill2", "zge")

    if F_SPARSE:
        xe_tok = [sb(f"xe_tok{i}", [128, CT, D], BF16) for i in range(2)]
        cmb_g = [sb(f"cmb_g{i}", [128, CT, NE], F32) for i in range(2)]
        xe_feat = [sb(f"xe_feat{i}", [128, NCH, CAP], BF16) for i in range(2)]
        h_buf = sb("h_buf", [128, HID // 128, CAP], BF16)
        wsl = [sb(f"wsl{i}", [128, NCH, 1024], BF16) for i in range(3)]
        ge_stage = [sb(f"ge_stage{i}", [128, D], F32) for i in range(2)]

        pa = psum("pa", [128, 2048], F32)
        gps = psum("gps", [128, 1024], F32)
        pst3 = psum("pst3", [128, 1024], BF16)

        gi_ld = r.val("d_gi")
        for i in range(2):
            r.op("vector", lambda v, i=i: v.memset(
                xe_tok[i][:].rearrange("p t d -> p (t d)"), 0.0), chain=True)
            r.op("vector", lambda v, i=i: v.memset(
                cmb_g[i][:].rearrange("p t c -> p (t c)"), 0.0), chain=True)
        xz = r.val("ch_dve")

        xe_gath = {}
        tr_done = {}
        fe_done = {}

        def gather_e(e, waits):
            par = e % 2
            w = list(waits) + [("d_gi", gi_ld)]
            for st in range(CT):
                col = e * CT + st
                r.op("gpsimd", lambda g, par=par, st=st, col=col:
                     g.indirect_dma_start(
                         out=xe_tok[par][:, st, :], out_offset=None,
                         in_=scr_xnt,
                         in_offset=bass.IndirectOffsetOnAxis(
                             gsrc_sb[:, col:col + 1], 0),
                         ),
                     waits=w if st == 0 else (),
                     incs=[(f"d_xe{par}", 16)])
                r.op("gpsimd", lambda g, par=par, st=st, col=col:
                     g.indirect_dma_start(
                         out=cmb_g[par][:, st, :], out_offset=None,
                         in_=scr_cmb,
                         in_offset=bass.IndirectOffsetOnAxis(
                             gsrc_sb[:, col:col + 1], 0),
                         ),
                     incs=[(f"d_xe{par}", 16)])
            xe_gath[e] = r.val(f"d_xe{par}")

        def transpose_e(e):
            par = e % 2
            r.wait("tensor", [(f"d_xe{par}", xe_gath[e]),
                              ("ch_dve", fe_done.get(e - 2, xz))])
            for st in range(CT):
                if st >= 1:
                    r.wait("tensor", [("ch_dve", r.val("ch_dve"))])
                for c in range(NCH):
                    r.op("tensor", lambda te, par=par, st=st, c=c:
                         te.transpose(
                             pst3[:, c * 128:(c + 1) * 128],
                             xe_tok[par][:, st, c * 128:(c + 1) * 128],
                             ident16[:]))
                ms = r.milestone("tensor")
                r.op("vector", lambda v, par=par, st=st: v.tensor_copy(
                    xe_feat[par][:, :, st * 128:(st + 1) * 128],
                    pst3[:].rearrange("p (c n) -> p c n", c=NCH)),
                    waits=[("ch_pe", ms)], chain=True)
            tr_done[e] = r.val("ch_pe")
            fe_done[e] = r.val("ch_dve")

        hsil_done = {}
        hmul_done = {}
        wsl_dma = [0, 0, 0]
        slot_free = [0, 0, 0]
        gb_dma = {}

        def wl(slot, src_ap):
            r.op("sync", lambda sy, slot=slot, src_ap=src_ap: sy.dma_start(
                out=wsl[slot][:], in_=src_ap),
                waits=[("ch_pe", slot_free[slot])],
                incs=[(f"d_ws{slot}", 16)])
            wsl_dma[slot] = r.val(f"d_ws{slot}")

        def h_phase_e(e, wh, kind, slot):
            par0 = e % 2
            r.wait("tensor", [(f"d_ws{slot}", wsl_dma[slot]),
                              ("ch_dve", fe_done[e])])
            for hc8 in range(NCH):
                hc = wh * NCH + hc8
                for gi, (t0, t1) in enumerate(ETG):
                    par = (hc8 * len(ETG) + gi) % 2
                    r.wait("tensor", [("ch_act", hsil_done.get(par, 0)),
                                      ("ch_dve", hmul_done.get(par, 0))])
                    for c in range(NCH):
                        r.op("tensor", lambda te, c=c, hc8=hc8, t0=t0, t1=t1,
                             par=par, slot=slot, par0=par0: te.matmul(
                                 pa[:, par * 1024:par * 1024 + (t1 - t0)],
                                 wsl[slot][:, c, hc8 * 128:(hc8 + 1) * 128],
                                 xe_feat[par0][:, c, t0:t1],
                                 start=(c == 0), stop=(c == NCH - 1)))
                    hmm = r.milestone("tensor")
                    if kind == 1:
                        r.op("scalar", lambda s, hc=hc, t0=t0, t1=t1, par=par:
                             s.activation(
                                 h_buf[:, hc, t0:t1],
                                 pa[:, par * 1024:par * 1024 + (t1 - t0)],
                                 AF.Silu),
                             waits=[("ch_pe", hmm)], chain=True)
                        hsil_done[par] = r.val("ch_act")
                    else:
                        r.op("vector", lambda v, hc=hc, t0=t0, t1=t1, par=par:
                             v.tensor_tensor(
                                 h_buf[:, hc, t0:t1], h_buf[:, hc, t0:t1],
                                 pa[:, par * 1024:par * 1024 + (t1 - t0)],
                                 ALU.mult),
                             waits=[("ch_pe", hmm)], chain=True)
                        hmul_done[par] = r.val("ch_dve")
            slot_free[slot] = r.milestone("tensor")

        def g_phase_e(e, s1, s2):
            par0 = e % 2
            hdone = r.val("ch_dve")
            r.wait("tensor", [(f"d_ws{s1}", wsl_dma[s1]),
                              (f"d_ws{s2}", wsl_dma[s2]),
                              ("ch_dve", hdone)])
            for st in range(CT):
                sh = st % 2
                for mh in range(2):
                    gq = (st * 2 + mh) % 2
                    gw = [("ch_dve", gacc_done.get(gq, 0))]
                    if mh == 0 and sh in gb_dma:
                        gw.append(("d_gb", gb_dma[sh]))
                    r.wait("tensor", gw)
                    for hc in range(HID // 128):
                        slot = s1 if hc < NCH else s2
                        r.op("tensor", lambda te, st=st, mh=mh, hc=hc,
                             gq=gq, slot=slot: te.matmul(
                                 gps[:, gq * 512:(gq + 1) * 512],
                                 h_buf[:, hc, st * 128:(st + 1) * 128],
                                 wsl[slot][:, hc % NCH,
                                           mh * 512:(mh + 1) * 512],
                                 start=(hc == 0),
                                 stop=(hc == HID // 128 - 1)))
                    gmm = r.milestone("tensor")
                    r.op("vector", lambda v, st=st, mh=mh, e=e, gq=gq,
                         sh=sh, par0=par0: v.tensor_scalar(
                             ge_stage[sh][:, mh * 512:(mh + 1) * 512],
                             gps[:, gq * 512:(gq + 1) * 512],
                             cmb_g[par0][:, st, e:e + 1], None, ALU.mult),
                         waits=[("ch_pe", gmm)], chain=True)
                    gacc_done[gq] = r.val("ch_dve")
                sdone = r.val("ch_dve")
                r.op("gpsimd", lambda g, e=e, st=st, sh=sh:
                     g.indirect_dma_start(
                         out=scr_ge, out_offset=bass.IndirectOffsetOnAxis(
                             gdst_sb[:, e * CT + st:e * CT + st + 1], 0),
                         in_=ge_stage[sh][:], in_offset=None),
                     waits=[("ch_dve", sdone), ("d_ge", ge_zeroed),
                            ("d_gb", r.val("d_gb"))],
                     incs=[("d_gb", 16)])
                gb_dma[sh] = r.val("d_gb")
            ms = r.milestone("tensor")
            slot_free[s1] = ms
            slot_free[s2] = ms

        gacc_done = {}
        r.wait("sync", [("d_wc", wc_done)])
        gather_e(0, [("ch_dve", xz)])
        gather_e(1, [])
        transpose_e(0)
        wl(0, scr_w1[0, :, :, 0:1024])
        wl(1, scr_w3[0, :, :, 0:1024])
        wl(2, scr_w1[0, :, :, 1024:2048])
        for e in range(NE):
            h_phase_e(e, 0, 1, 0)
            wl(0, scr_w3[e, :, :, 1024:2048])
            h_phase_e(e, 0, 3, 1)
            wl(1, scr_w2[e, :, 0:NCH, :])
            h_phase_e(e, 1, 1, 2)
            wl(2, scr_w2[e, :, NCH:2 * NCH, :])
            h_phase_e(e, 1, 3, 0)
            if e + 1 < NE:
                wl(0, scr_w1[e + 1, :, :, 0:1024])
                transpose_e(e + 1)
            g_phase_e(e, 1, 2)
            if e + 2 < NE:
                gather_e(e + 2, [("ch_pe", tr_done[e])])
            if e + 1 < NE:
                wl(1, scr_w3[e + 1, :, :, 0:1024])
                wl(2, scr_w1[e + 1, :, :, 1024:2048])

        r.wait("sync", [(x, r.val(x)) for x in
                        ("d_ws0", "d_ws1", "d_ws2", "d_gi", "d_xe0",
                         "d_xe1", "d_gb", "d_wc")])
        blocks.append(("P3", dict(r.ops)))
        r.ops = {e: [] for e in Rec.ENGS}
        close("xe_tok0", "xe_tok1", "cmb_g0",
              "cmb_g1", "xe_feat0", "xe_feat1", "h_buf", "wsl0", "wsl1",
              "wsl2", "ge_stage0", "ge_stage1", "pa", "gps", "pst3")
    else:
        combine = sb("combine", [128, OWN_QT * NE], F32)
        g_sb = sb("g_sb", [128, OWN_QT, D], F32)
        h_buf = sb("h_buf", [128, HID // 128, TH], BF16)
        wsl = [sb(f"wsl{i}", [128, NCH, 1024], BF16) for i in range(3)]
        stmp = sb("stmp", [128, 1024], BF16)
        gm = sb("gm", [128, 4 * NE], F32)

        pa = psum("pa", [128, 2048], F32)
        gps = psum("gps", [128, 1024], F32)

        r.op("vector", lambda v: v.memset(g_sb[:].rearrange("p t d -> p (t d)"),
                                          0.0), chain=True)

        for tt in range(OWN_QT):
            sc = scores[:, tt * NE:(tt + 1) * NE]
            r.op("vector", lambda v, sc=sc: v.reduce_max(
                sm[:, 0:1], sc, axis=AX.X),
                waits=[("ch_act", r.val("ch_act"))], chain=True)
            r.op("vector", lambda v, sc=sc: v.tensor_scalar(
                gm[:, 0:NE], sc, sm[:, 0:1], None, ALU.is_equal),
                chain=True)
            r.op("vector", lambda v, sc=sc: v.scalar_tensor_tensor(
                gm[:, NE:2 * NE], gm[:, 0:NE], -1e9, sc,
                ALU.mult, ALU.add), chain=True)
            r.op("vector", lambda v: v.reduce_max(
                sm[:, 1:2], gm[:, NE:2 * NE], axis=AX.X), chain=True)
            r.op("vector", lambda v: v.tensor_scalar(
                gm[:, 2 * NE:3 * NE], gm[:, NE:2 * NE], sm[:, 1:2], None,
                ALU.is_equal), chain=True)
            r.op("vector", lambda v: v.tensor_sub(
                sm[:, 2:3], sm[:, 0:1], sm[:, 1:2]), chain=True)
            dcol = r.val("ch_dve")
            r.op("scalar", lambda s: s.activation(
                sm[:, 3:4], sm[:, 2:3], AF.Sigmoid),
                waits=[("ch_dve", dcol)], chain=True)
            r.op("scalar", lambda s: s.activation(
                sm[:, 4:5], sm[:, 2:3], AF.Sigmoid, scale=-1.0), chain=True)
            sig = r.val("ch_act")
            r.op("vector", lambda v: v.tensor_scalar(
                gm[:, 3 * NE:4 * NE], gm[:, 0:NE], sm[:, 3:4], None, ALU.mult),
                waits=[("ch_act", sig)], chain=True)
            r.op("vector", lambda v, tt=tt: v.scalar_tensor_tensor(
                combine[:, tt * NE:(tt + 1) * NE], gm[:, 2 * NE:3 * NE],
                sm[:, 4:5], gm[:, 3 * NE:4 * NE], ALU.mult, ALU.add),
                chain=True)
        gate_done = r.val("ch_dve")

        hmul_done = {}
        gacc_done = {}
        wsl_dma = [0, 0, 0]
        slot_free = [0, 0, 0]

        def wl(slot, src_ap):
            r.op("gpsimd", lambda g, slot=slot, src_ap=src_ap: g.dma_start(
                out=wsl[slot][:], in_=src_ap),
                waits=[("ch_pe", slot_free[slot])], incs=[(f"d_ws{slot}", 16)])
            wsl_dma[slot] = r.val(f"d_ws{slot}")

        def h_phase(e, th, wh, kind, slot):
            tok0 = th * TH
            r.wait("tensor", [(f"d_ws{slot}", wsl_dma[slot]),
                              ("ch_dve", gate_done)])
            for hc8 in range(NCH):
                hc = wh * NCH + hc8
                for tg in range(TH // TGS):
                    par = (hc8 * (TH // TGS) + tg) % 2
                    r.wait("tensor", [("ch_act", hsil_done.get(par, 0)),
                                      ("ch_dve", hmul_done.get(par, 0))])
                    for c in range(NCH):
                        r.op("tensor", lambda te, c=c, hc8=hc8, tg=tg, tok0=tok0,
                             par=par, slot=slot: te.matmul(
                                 pa[:, par * 1024:par * 1024 + TGS],
                                 wsl[slot][:, c, hc8 * 128:(hc8 + 1) * 128],
                                 xn_own[:, c, tok0 + tg * TGS:
                                        tok0 + (tg + 1) * TGS],
                                 start=(c == 0), stop=(c == NCH - 1)))
                    hmm = r.milestone("tensor")
                    if kind == 1:
                        r.op("scalar", lambda s, hc=hc, tg=tg, par=par:
                             s.activation(
                                 h_buf[:, hc, tg * TGS:(tg + 1) * TGS],
                                 pa[:, par * 1024:par * 1024 + TGS], AF.Silu),
                             waits=[("ch_pe", hmm)], chain=True)
                        hsil_done[par] = r.val("ch_act")
                    else:
                        r.op("vector", lambda v, hc=hc, tg=tg, par=par:
                             v.tensor_tensor(
                                 h_buf[:, hc, tg * TGS:(tg + 1) * TGS],
                                 h_buf[:, hc, tg * TGS:(tg + 1) * TGS],
                                 pa[:, par * 1024:par * 1024 + TGS],
                                 ALU.mult),
                             waits=[("ch_pe", hmm)], chain=True)
                        hmul_done[par] = r.val("ch_dve")
            slot_free[slot] = r.milestone("tensor")

        def g_phase(e, th, s1, s2):
            hdone = r.val("ch_dve")
            r.wait("tensor", [(f"d_ws{s1}", wsl_dma[s1]),
                              (f"d_ws{s2}", wsl_dma[s2]), ("ch_dve", hdone)])
            for tt8 in range(TH // 128):
                tt = th * (TH // 128) + tt8
                for mh in range(2):
                    gq = (tt8 * 2 + mh) % 2
                    r.wait("tensor", [("ch_dve", gacc_done.get(gq, 0))])
                    for hc in range(HID // 128):
                        slot = s1 if hc < NCH else s2
                        r.op("tensor", lambda te, tt8=tt8, mh=mh, hc=hc,
                             gq=gq, slot=slot: te.matmul(
                                 gps[:, gq * 512:(gq + 1) * 512],
                                 h_buf[:, hc, tt8 * 128:(tt8 + 1) * 128],
                                 wsl[slot][:, hc % NCH, mh * 512:(mh + 1) * 512],
                                 start=(hc == 0), stop=(hc == HID // 128 - 1)))
                    gmm = r.milestone("tensor")
                    r.op("vector", lambda v, tt=tt, mh=mh, e=e, gq=gq:
                         v.scalar_tensor_tensor(
                             g_sb[:, tt, mh * 512:(mh + 1) * 512],
                             gps[:, gq * 512:(gq + 1) * 512],
                             combine[:, tt * NE + e:tt * NE + e + 1],
                             g_sb[:, tt, mh * 512:(mh + 1) * 512],
                             ALU.mult, ALU.add),
                         waits=[("ch_pe", gmm)], chain=True)
                    gacc_done[gq] = r.val("ch_dve")
            ms = r.milestone("tensor")
            slot_free[s1] = ms
            slot_free[s2] = ms

        hsil_done = {}
        wl(0, scr_w1[0, :, :, 0:1024])
        wl(1, scr_w3[0, :, :, 0:1024])
        wl(2, scr_w1[0, :, :, 1024:2048])
        for e in range(NE):
            for th in range(2):
                h_phase(e, th, 0, 1, 0)
                wl(0, scr_w3[e, :, :, 1024:2048])
                h_phase(e, th, 0, 3, 1)
                wl(1, scr_w2[e, :, 0:NCH, :])
                h_phase(e, th, 1, 1, 2)
                wl(2, scr_w2[e, :, NCH:2 * NCH, :])
                h_phase(e, th, 1, 3, 0)
                ne, nth = (e, th + 1) if th == 0 else (e + 1, 0)
                if ne < NE:
                    wl(0, scr_w1[ne, :, :, 0:1024])
                g_phase(e, th, 1, 2)
                if ne < NE:
                    wl(1, scr_w3[ne, :, :, 0:1024])
                    wl(2, scr_w1[ne, :, :, 1024:2048])

        r.wait("sync", [(x, r.val(x)) for x in
                        ("d_ws0", "d_ws1", "d_ws2")])
        blocks.append(("P3", dict(r.ops)))
        r.ops = {e: [] for e in Rec.ENGS}

        close("h_buf", "wsl0", "wsl1", "wsl2", "stmp", "gm", "pa", "gps",
              "combine", "g_sb")

    xo_buf = [sb(f"xo_buf{i}", [128, D], F32) for i in range(2)]
    sv_buf = [sb(f"sv_buf{i}", [128, D], BF16) for i in range(2)]
    sg_buf = sb("sg_buf", [128, D], F32)
    o_stage = [sb(f"o_stage{i}", [128, D], F32) for i in range(2)]
    if F_SPARSE:
        ge_buf = [sb(f"ge_buf{i}", [128, 2, D], F32) for i in range(2)]
    out_dma = {}
    fin_done = {}
    gadd_done = {}
    dgb_final = r.val("d_gb") if F_SPARSE else 0
    for qt in range(OWN_QT):
        pi = qt % 2
        lw = []
        if qt >= 2:
            lw.append(("ch_dve", fin_done[qt - 2]))
        r.op("sync", lambda sy, qt=qt, pi=pi: sy.dma_start(
            out=xo_buf[pi][:], in_=xo_in[qt * 128:(qt + 1) * 128, :]),
            waits=lw, incs=[(f"d_x{pi}", 16)])
        r.op("sync", lambda sy, qt=qt, pi=pi: sy.dma_start(
            out=sv_buf[pi][:], in_=scr_s[qt, :, :]),
            waits=[], incs=[(f"d_x{pi}", 16)])
        ld = r.val(f"d_x{pi}")
        if F_SPARSE:
            gw = ([("ch_dve", gadd_done[qt - 2])] if qt >= 2
                  else [("d_gb", dgb_final)])
            r.op("gpsimd", lambda g, qt=qt, pi=pi: g.dma_start(
                out=ge_buf[pi][:, 0, :],
                in_=scr_ge[qt * 128:(qt + 1) * 128, :]),
                waits=gw, incs=[(f"d_g{pi}", 16)])
            r.op("gpsimd", lambda g, qt=qt, pi=pi: g.dma_start(
                out=ge_buf[pi][:, 1, :],
                in_=scr_ge[OWNQ + qt * 128:OWNQ + (qt + 1) * 128, :]),
                incs=[(f"d_g{pi}", 16)])
            ldg = r.val(f"d_g{pi}")
            r.op("vector", lambda v, pi=pi: v.tensor_add(
                ge_buf[pi][:, 0, :], ge_buf[pi][:, 0, :],
                ge_buf[pi][:, 1, :]),
                waits=[(f"d_g{pi}", ldg)], chain=True)
            gadd_done[qt] = r.val("ch_dve")
            r.op("scalar", lambda s, pi=pi: s.activation(
                sg_buf[:], ge_buf[pi][:, 0, :], AF.Silu),
                waits=[("ch_dve", gadd_done[qt])], chain=True)
        else:
            r.op("scalar", lambda s, qt=qt: s.activation(
                sg_buf[:], g_sb[:, qt, :], AF.Silu), chain=True)
        sg = r.val("ch_act")
        ow = [("ch_act", sg), (f"d_x{pi}", ld)]
        if qt >= 2:
            ow.append((f"d_o{pi}", out_dma[qt - 2]))
        r.op("vector", lambda v, pi=pi: v.tensor_tensor(
            o_stage[pi][:], sv_buf[pi][:], sg_buf[:], ALU.mult),
            waits=ow, chain=True)
        r.op("vector", lambda v, pi=pi: v.tensor_add(
            o_stage[pi][:], o_stage[pi][:], xo_buf[pi][:]), chain=True)
        fin_done[qt] = r.val("ch_dve")
        r.op("scalar", lambda s, qt=qt, pi=pi: s.dma_start(
            out=out_t[qt * 128:(qt + 1) * 128, :], in_=o_stage[pi][:]),
            waits=[("ch_dve", fin_done[qt])], incs=[(f"d_o{pi}", 16)])
        out_dma[qt] = r.val(f"d_o{pi}")

    r.wait("sync", [(x, r.val(x)) for x in
                    ("d_o0", "d_o1", "d_x0", "d_x1", "d_g0", "d_g1")
                    if r.val(x) > 0])
    blocks.append(("P4", dict(r.ops)))
    r.ops = {e: [] for e in Rec.ENGS}

    semnames = list(r.cnt.keys())
    sems = {}
    semctx = []
    for s in semnames:
        cm = nc.semaphore(s)
        sems[s] = cm.__enter__()
        semctx.append(cm)

    for name, ops in (blocks if nblocks is None else blocks[:nblocks]):
        rb = Rec()
        rb.ops = ops
        emit_block(nc, rb, sems, name)

    return nc


def emit_block(nc, rec, sems, name):
    todo = {e: rec.ops[e] for e in Rec.ENGS if rec.ops[e]}
    if not todo:
        return
    with nc.Block(name) as blk:
        for ename, oplist in todo.items():
            def mk(oplist):
                def body(eng):
                    for waits, fn, incs in oplist:
                        for s, v in waits:
                            if v > 0:
                                eng.wait_ge(sems[s], v)
                        if fn is None:
                            continue
                        inst = fn(eng)
                        for s, n in incs:
                            inst.then_inc(sems[s], n)
                return body
            getattr(blk, ename)(mk(oplist))


def _host_inputs(core, X, Wq, Wk, Wv, Wo, bQ, bK, bV, bO, Wg, W1, W3, W2,
                 seq):
    NT = seq // 128
    NM = NT // 8
    b, j = core // 4, core % 4
    qtiles = []
    for m in range(NM):
        qtiles += [8 * m + j, 8 * m + 7 - j]

    xb = np.ascontiguousarray(X[b])
    x_own = np.ascontiguousarray(
        np.concatenate([xb[t * 128:(t + 1) * 128] for t in qtiles], axis=0))

    def packQK(Wt):
        w4 = Wt.reshape(NP, 2, NCH, 128, DH)
        return np.ascontiguousarray(w4.transpose(3, 0, 2, 1, 4)
                                    .reshape(128, NP, NCH, 128))

    wq = packQK(Wq)
    wk = packQK(Wk)
    wv = np.ascontiguousarray(
        Wv.reshape(H, NCH, 128, DH).transpose(2, 1, 0, 3)
        .reshape(128, NCH, H * DH))
    bq = np.ascontiguousarray(
        bQ.reshape(NP, 2, DH).transpose(1, 2, 0).reshape(128, NP))
    bk = np.ascontiguousarray(
        bK.reshape(NP, 2, DH).transpose(1, 2, 0).reshape(128, NP))
    bv = np.ascontiguousarray(
        bV.reshape(NP, 2, DH).transpose(1, 2, 0).reshape(128, NP))
    wo = np.ascontiguousarray(
        Wo.reshape(NCH, 128, D).transpose(1, 0, 2))
    bo = np.ascontiguousarray(bO.reshape(1, D))
    wg = np.ascontiguousarray(
        Wg.reshape(NCH, 128, NE).transpose(1, 0, 2))
    w1 = np.ascontiguousarray(
        W1.reshape(NE, NCH, 128, HID).transpose(0, 2, 1, 3))
    w3 = np.ascontiguousarray(
        W3.reshape(NE, NCH, 128, HID).transpose(0, 2, 1, 3))
    w2 = np.ascontiguousarray(
        W2.reshape(NE, HID // 128, 128, D).transpose(0, 2, 1, 3))

    kk = np.arange(128)[:, None]
    qq = np.arange(128)[None, :]
    pmask = np.zeros((NM, 12, 128, 128), dtype=np.float32)
    for m in range(NM):
        t0 = 8 * m + j
        t1 = 8 * m + 7 - j
        for i in range(8):
            kt = 8 * m + i
            if m == 0:
                pmask[m, i] = 0.0
            else:
                pmask[m, i] = (128 * kt + kk <= 128 * t0 + qq)
        for i in range(4):
            kt = 8 * m + 4 + i
            pmask[m, 8 + i] = (128 * kt + kk <= 128 * t1 + qq)
    pmask = np.ascontiguousarray(
        pmask.transpose(2, 0, 1, 3).reshape(128, NM * 12 * 128)
    ).astype(ml_dtypes.bfloat16)

    kk0 = np.arange(Q0KT * 128)[None, :]
    qq0 = np.arange(128)[:, None]
    pmask0 = np.where(kk0 <= 128 * j + qq0, 0.0, -30000.0).astype(np.float32)

    return dict(x=xb, x_own=x_own, wq=wq, wk=wk, wv=wv, bq=bq, bk=bk, bv=bv,
                wo=wo, bo=bo, wg=wg, w1=w1, w3=w3, w2=w2,
                pmask=pmask, pmask0=pmask0), qtiles


def kernel(L0_attn_residual, L0_attn_W_Q, L0_attn_W_K, L0_attn_W_V,
           L0_attn_W_O, L0_attn_b_Q, L0_attn_b_K, L0_attn_b_V, L0_attn_b_O,
           L0_ffn_Wg, L0_ffn_W1, L0_ffn_W3, L0_ffn_W2):
    X = np.asarray(L0_attn_residual, dtype=np.float32)
    seq = X.shape[1]
    nc = build(seq)

    in_maps = []
    all_qtiles = []
    for c in range(8):
        m, qt = _host_inputs(
            c, X, np.asarray(L0_attn_W_Q, np.float32),
            np.asarray(L0_attn_W_K, np.float32),
            np.asarray(L0_attn_W_V, np.float32),
            np.asarray(L0_attn_W_O, np.float32),
            np.asarray(L0_attn_b_Q, np.float32),
            np.asarray(L0_attn_b_K, np.float32),
            np.asarray(L0_attn_b_V, np.float32),
            np.asarray(L0_attn_b_O, np.float32),
            np.asarray(L0_ffn_Wg, np.float32),
            np.asarray(L0_ffn_W1, np.float32),
            np.asarray(L0_ffn_W3, np.float32),
            np.asarray(L0_ffn_W2, np.float32), seq)
        in_maps.append(m)
        all_qtiles.append(qt)

    res = run_bass_kernel_spmd(nc, in_maps, core_ids=list(range(8)))

    out = np.empty((B, seq, D), dtype=np.float32)
    for c in range(8):
        b = c // 4
        o = res.results[c]["out"]
        for i, t in enumerate(all_qtiles[c]):
            out[b, t * 128:(t + 1) * 128, :] = o[i * 128:(i + 1) * 128, :]
    return out


# version 23
# speedup vs baseline: 1.1071x; 1.1071x over previous
import math
import numpy as np
import ml_dtypes

import concourse.bass as bass
import concourse.mybir as mybir
from concourse.bass_utils import run_bass_kernel_spmd

F32 = mybir.dt.float32
BF16 = mybir.dt.bfloat16
AF = mybir.ActivationFunctionType
ALU = mybir.AluOpType
AX = mybir.AxisListType

B = 2
D = 1024
H = 16
DH = 64
NP = 8
NCH = 8
HID = 2048
NE = 8
SCALE = 1.0 / math.sqrt(DH)
EPS = 1e-5
Q0KT = 4


class Rec:
    ENGS = ("sync", "tensor", "vector", "scalar", "gpsimd")
    CHAIN = {"vector": "ch_dve", "scalar": "ch_act", "gpsimd": "ch_pool",
             "tensor": "ch_pe"}

    def __init__(self):
        self.ops = {e: [] for e in self.ENGS}
        self.cnt = {}

    def sem(self, name):
        self.cnt.setdefault(name, 0)
        return name

    def val(self, s):
        return self.cnt.get(s, 0)

    def op(self, eng, fn, waits=(), incs=(), chain=False):
        w = list(waits)
        i = list(incs)
        if chain:
            ch = self.CHAIN[eng]
            self.sem(ch)
            w.append((ch, self.cnt[ch]))
            i.append((ch, 1))
        for s, _ in w:
            self.sem(s)
        for s, n in i:
            self.sem(s)
            self.cnt[s] += n
        self.ops[eng].append((w, fn, i))

    def wait(self, eng, waits):
        for s, _ in waits:
            self.sem(s)
        self.ops[eng].append((list(waits), None, []))

    def milestone(self, eng, sem="ch_pe"):
        self.sem(sem)
        for idx in range(len(self.ops[eng]) - 1, -1, -1):
            w, fn, incs = self.ops[eng][idx]
            if fn is None:
                continue
            for sname, _ in incs:
                if sname == sem:
                    return self.cnt[sem]
            if incs:
                self.cnt[sem] += 1
                self.ops[eng].append(([], lambda e: e.nop(), [(sem, 1)]))
                return self.cnt[sem]
            incs.append((sem, 1))
            self.cnt[sem] += 1
            return self.cnt[sem]
        raise RuntimeError("no real op to attach milestone")


def build(seq=8192, nblocks=None, feat=None):
    feat = feat or {}
    F_SAMPLE = feat.get("sample", True)
    F_MASK = feat.get("mask", True)
    F_EXACT = feat.get("exact", True)
    F_DANCE = feat.get("dance", True)
    F_SPARSE = feat.get("sparse", True)
    NT = seq // 128
    NW = seq // 512
    NM = NT // 8
    OWNQ = NM * 256
    OWN_QT = OWNQ // 128
    TH = OWNQ // 2
    TGS = min(512, TH)
    QB = max(1, OWNQ // 512)
    CAP = 640 if OWNQ >= 2048 else 256
    CT = CAP // 128
    ETG = [(0, min(CAP, 512))] + ([(512, CAP)] if CAP > 512 else [])
    OOB = 1 << 20

    nc = bass.Bass("TRN2", target_bir_lowering=False)

    def dram(name, shape, dtype, kind):
        return nc.dram_tensor(name, shape, dtype, kind=kind).ap()

    x_in = dram("x", [seq, D], F32, "ExternalInput")
    xo_in = dram("x_own", [OWNQ, D], F32, "ExternalInput")
    wq_in = dram("wq", [128, NP, NCH, 128], F32, "ExternalInput")
    wk_in = dram("wk", [128, NP, NCH, 128], F32, "ExternalInput")
    wv_in = dram("wv", [128, NCH, H * DH], F32, "ExternalInput")
    bq_in = dram("bq", [128, NP], F32, "ExternalInput")
    bk_in = dram("bk", [128, NP], F32, "ExternalInput")
    bv_in = dram("bv", [128, NP], F32, "ExternalInput")
    wo_in = dram("wo", [128, NCH, D], F32, "ExternalInput")
    bo_in = dram("bo", [1, D], F32, "ExternalInput")
    wg_in = dram("wg", [128, NCH, NE], F32, "ExternalInput")
    w1_in = dram("w1", [NE, 128, NCH, HID], F32, "ExternalInput")
    w3_in = dram("w3", [NE, 128, NCH, HID], F32, "ExternalInput")
    w2_in = dram("w2", [NE, 128, HID // 128, D], F32, "ExternalInput")
    scr_w1 = dram("scr_w1", [NE, 128, NCH, HID], BF16, "Internal")
    scr_w3 = dram("scr_w3", [NE, 128, NCH, HID], BF16, "Internal")
    scr_w2 = dram("scr_w2", [NE, 128, HID // 128, D], BF16, "Internal")
    pm_in = dram("pmask", [128, NM * 12 * 128], BF16, "ExternalInput")
    pm0_in = dram("pmask0", [128, Q0KT * 128], F32, "ExternalInput")

    out_t = dram("out", [OWNQ, D], F32, "ExternalOutput")

    scr_k = dram("scr_k", [NP, 128, seq], BF16, "Internal")
    scr_xnt = dram("scr_xnt", [OWNQ, D], BF16, "Internal")
    scr_cmb = dram("scr_cmb", [OWNQ, NE], F32, "Internal")
    NSLOT = NE * CAP
    SDUMP = 2 * NSLOT + 16384
    scr_gsrc = dram("scr_gsrc", [SDUMP, 1], mybir.dt.int32, "Internal")
    scr_gdst = dram("scr_gdst", [SDUMP, 1], mybir.dt.int32, "Internal")
    scr_ge = dram("scr_ge", [2 * OWNQ + 128, D], F32, "Internal")
    scr_v = dram("scr_v", [NT, 128, NP * 130], BF16, "Internal")
    scr_s = dram("scr_s", [OWN_QT, 128, D], BF16, "Internal")

    r = Rec()
    ctxs = []

    def sb(name, shape, dtype):
        cm = nc.sbuf_tensor(name, shape, dtype)
        h = cm.__enter__()
        ctxs.append([name, cm])
        return h

    def psum(name, shape, dtype):
        cm = nc.psum_tensor(name, shape, dtype)
        h = cm.__enter__()
        ctxs.append([name, cm])
        return h

    def close(*names):
        ns = set(names)
        for item in reversed(list(ctxs)):
            if item[0] in ns:
                item[1].__exit__(None, None, None)
                ctxs.remove(item)

    ident16 = sb("ident16", [128, 128], BF16)
    ident32 = sb("ident32", [128, 128], F32)
    ones16 = sb("ones16", [1, 128], BF16)
    scores = sb("scores", [128, OWN_QT * NE], F32)
    bvt = sb("bvt", [128, NP], F32)
    bqp = sb("bqp", [128, NP], F32)
    bkp = sb("bkp", [128, NP], F32)
    bo_sb = sb("bo_sb", [1, D], BF16)
    sm = sb("sm", [128, 24], F32)
    sm16 = sb("sm16", [128, 4], BF16)
    gsrc_sb = sb("gsrc_sb", [128, NE * CAP // 128], mybir.dt.int32)
    gdst_sb = sb("gdst_sb", [128, NE * CAP // 128], mybir.dt.int32)
    q_sb = sb("q_sb", [128, NP, OWNQ], BF16)
    xn_own = sb("xn_own", [128, NCH, OWNQ], BF16)

    r.op("gpsimd", lambda g: g.memset(ident16[:], 1.0), chain=True)
    r.op("gpsimd", lambda g: g.affine_select(
        ident16[:], ident16[:], [[-1, 128]], ALU.is_equal, 0.0,
        channel_multiplier=1), chain=True)
    r.op("gpsimd", lambda g: g.memset(ident32[:], 1.0), chain=True)
    r.op("gpsimd", lambda g: g.affine_select(
        ident32[:], ident32[:], [[-1, 128]], ALU.is_equal, 0.0,
        channel_multiplier=1), chain=True)
    r.op("gpsimd", lambda g: g.memset(ones16[:], 1.0), chain=True)
    consts_done = r.val("ch_pool")

    wq_sb = sb("wq_sb", [128, NP * NCH, 128], BF16)
    wk_sb = sb("wk_sb", [128, NP * NCH, 128], BF16)
    wv_sb = sb("wv_sb", [128, NCH, H * DH], BF16)
    wg_sb = sb("wg_sb", [128, NCH, NE], BF16)

    for fn in (
        lambda g: g.dma_start(
            out=wq_sb[:].rearrange("p (a c) n -> p a c n", a=NP), in_=wq_in),
        lambda g: g.dma_start(
            out=wk_sb[:].rearrange("p (a c) n -> p a c n", a=NP), in_=wk_in),
        lambda g: g.dma_start(out=wv_sb[:], in_=wv_in),
        lambda g: g.dma_start(out=wg_sb[:], in_=wg_in),
        lambda g: g.dma_start(out=bvt[:], in_=bv_in),
        lambda g: g.dma_start(out=bqp[:], in_=bq_in),
        lambda g: g.dma_start(out=bkp[:], in_=bk_in),
        lambda g: g.dma_start(out=bo_sb[:], in_=bo_in),
    ):
        r.op("gpsimd", fn, incs=[("d_w", 16)])
    w0_loads = r.val("d_w")

    x_buf = [sb(f"x_buf{i}", [128, 4, D], F32) for i in range(2)]
    xn_tok = [sb(f"xn_tok{i}", [128, D], BF16) for i in range(2)]
    xn_win = [sb(f"xn_win{i}", [128, NCH, 512], BF16) for i in range(2)]
    v_stage = [sb(f"v_stage{i}", [128, NP * 130], BF16) for i in range(2)]
    k_stage = [sb(f"k_stage{i}", [128, NP, 512], BF16) for i in range(2)]
    xsq2 = [sb(f"xsq2_{i}", [128, D], F32) for i in range(2)]

    ps_t = psum("ps_t", [128, 2048], BF16)
    ps_v = psum("ps_v", [128, 2048], F32)
    ps_k = psum("ps_k", [128, 1024], F32)

    for i in range(2):
        r.op("vector", lambda v, i=i: v.memset(v_stage[i][:], 1.0), chain=True)
    r.op("vector", lambda v: v.memset(sm[:, 23:24], EPS), chain=True)

    def ln_tile(xap, outap, pre_waits, cb=0):
        r.op("vector", lambda v: v.reduce_sum(
            sm[:, cb:cb + 1], xap(), axis=AX.X), waits=pre_waits, chain=True)
        r.op("scalar", lambda s: s.activation(
            xsq2[cb // 8][:], xap(), AF.Square,
            accum_out=sm[:, cb + 1:cb + 2]),
            waits=list(pre_waits), chain=True)
        sq = r.val("ch_act")
        r.op("vector", lambda v: v.tensor_scalar(
            sm[:, cb + 2:cb + 3], sm[:, cb:cb + 1], 1.0 / D, None, ALU.mult),
            chain=True)
        r.op("vector", lambda v: v.tensor_scalar(
            sm[:, cb + 3:cb + 4], sm[:, cb + 1:cb + 2], 1.0 / D, None,
            ALU.mult), waits=[("ch_act", sq)], chain=True)
        r.op("vector", lambda v: v.tensor_tensor(
            sm[:, cb + 4:cb + 5], sm[:, cb + 2:cb + 3], sm[:, cb + 2:cb + 3],
            ALU.mult), chain=True)
        r.op("vector", lambda v: v.tensor_sub(
            sm[:, cb + 5:cb + 6], sm[:, cb + 3:cb + 4],
            sm[:, cb + 4:cb + 5]), chain=True)
        vd = r.val("ch_dve")
        r.op("scalar", lambda s: s.activation(
            sm[:, cb + 6:cb + 7], sm[:, cb + 5:cb + 6], AF.Sqrt,
            bias=sm[:, 23:24]), waits=[("ch_dve", vd)], chain=True)
        sq2 = r.val("ch_act")
        r.op("vector", lambda v: v.reciprocal(
            sm[:, cb + 7:cb + 8], sm[:, cb + 6:cb + 7]),
            waits=[("ch_act", sq2)], chain=True)
        r.op("vector", lambda v: v.tensor_scalar(
            outap(), xap(), sm[:, cb + 2:cb + 3], sm[:, cb + 7:cb + 8],
            ALU.subtract, ALU.mult), chain=True)
        return r.val("ch_dve")

    y_done = {}
    z_done = {}
    n_done = {}
    tpos_done = {}
    e_done = {}
    v_done = {}
    ve_done = {}
    k_done = {}
    ke_done = {}
    vstage_dma = {}
    kstage_dma = {}
    xld_w = {}
    vinit = r.val("ch_dve")
    NT0 = NW * 4

    def emit_xdma(w):
        bi = w % 2
        lw = [("ch_dve", n_done[4 * (w - 2) + 3])] if w >= 2 else []
        r.op("sync", lambda sy, w=w, bi=bi: sy.dma_start(
            out=x_buf[bi][:],
            in_=x_in[w * 512:(w + 1) * 512, :]
            .rearrange("(t p) d -> p t d", p=128)),
            waits=lw, incs=[(f"d_x{bi}", 16)])
        xld_w[w] = r.val(f"d_x{bi}")

    emit_xdma(0)
    emit_xdma(1)

    for gt in range(NT0 + 4):
        w, t = divmod(gt, 4)
        if gt < NT0:
            bi = w % 2
            ti = gt % 2
            cb = ti * 8
            xap = lambda bi=bi, t=t: x_buf[bi][:, t, :]
            xw = [(f"d_x{bi}", xld_w[w])]
            if gt >= 2:
                xw.append(("ch_dve", y_done[gt - 2]))
            r.op("scalar", lambda sc, xap=xap, cb=cb, ti=ti: sc.activation(
                xsq2[ti][:], xap(), AF.Copy,
                accum_out=sm[:, cb:cb + 1]), waits=xw, chain=True)
            r.op("scalar", lambda sc, xap=xap, cb=cb, ti=ti: sc.activation(
                xsq2[ti][:], xap(), AF.Square,
                accum_out=sm[:, cb + 1:cb + 2]), chain=True)
            a_done = r.val("ch_act")

        g1 = gt - 1
        if 0 <= g1 < NT0:
            w1, t1 = divmod(g1, 4)
            bi1 = w1 % 2
            ti1 = g1 % 2
            tb1 = ti1 * 1024
            ptw = [("ch_dve", n_done[g1]), ("ch_pool", consts_done)]
            if g1 >= 2:
                ptw.append(("ch_dve", e_done[g1 - 2]))
            r.wait("tensor", ptw)
            for c in range(NCH):
                r.op("tensor", lambda te, ti1=ti1, c=c, tb1=tb1: te.transpose(
                    ps_t[:, tb1 + c * 128:tb1 + (c + 1) * 128],
                    xn_tok[ti1][:, c * 128:(c + 1) * 128], ident16[:]))
            tpos_done[g1] = r.milestone("tensor")
            r.op("vector", lambda v, bi1=bi1, t1=t1, tb1=tb1: v.tensor_copy(
                xn_win[bi1][:, :, t1 * 128:(t1 + 1) * 128],
                ps_t[:, tb1:tb1 + 1024].rearrange("p (c n) -> p c n",
                                                  c=NCH)),
                waits=[("ch_pe", tpos_done[g1])], chain=True)
            e_done[g1] = r.val("ch_dve")

        g2 = gt - 2
        if 0 <= g2 < NT0:
            w2, t2 = divmod(g2, 4)
            bi2 = w2 % 2
            vh2 = g2 % 2
            vw_ = [("ch_dve", e_done[g2])]
            if g2 < 2:
                vw_.append(("d_w", w0_loads))
            else:
                vw_.append(("ch_act", ve_done[g2 - 2]))
            r.wait("tensor", vw_)
            for half in range(2):
                for c in range(NCH):
                    r.op("tensor", lambda te, bi2=bi2, t2=t2, c=c, half=half,
                         vh2=vh2: te.matmul(
                             ps_v[:, vh2 * 1024 + half * 512:
                                  vh2 * 1024 + (half + 1) * 512],
                             xn_win[bi2][:, c, t2 * 128:(t2 + 1) * 128],
                             wv_sb[:, c, half * 512:(half + 1) * 512],
                             start=(c == 0), stop=(c == NCH - 1)))
            v_done[g2] = r.milestone("tensor")

        if gt < NT0:
            r.op("vector", lambda v, cb=cb: v.tensor_scalar(
                sm[:, cb + 2:cb + 3], sm[:, cb:cb + 1], 1.0 / D, None,
                ALU.mult),
                waits=[("ch_act", a_done)], chain=True)
            r.op("vector", lambda v, cb=cb: v.tensor_tensor(
                sm[:, cb + 4:cb + 5], sm[:, cb + 2:cb + 3],
                sm[:, cb + 2:cb + 3], ALU.mult), chain=True)
            r.op("vector", lambda v, cb=cb: v.tensor_scalar(
                sm[:, cb + 5:cb + 6], sm[:, cb + 1:cb + 2], 1.0 / D,
                sm[:, cb + 4:cb + 5], ALU.mult, ALU.subtract), chain=True)
            y_done[gt] = r.val("ch_dve")
            r.op("scalar", lambda sc, cb=cb: sc.activation(
                sm[:, cb + 6:cb + 7], sm[:, cb + 5:cb + 6], AF.Sqrt,
                bias=sm[:, 23:24]),
                waits=[("ch_dve", y_done[gt])], chain=True)
            z_done[gt] = r.val("ch_act")
            r.op("vector", lambda v, cb=cb: v.reciprocal(
                sm[:, cb + 7:cb + 8], sm[:, cb + 6:cb + 7]),
                waits=[("ch_act", z_done[gt])], chain=True)
            nw_ = [("ch_pe", tpos_done[gt - 2])] if gt >= 2 else []
            r.op("vector", lambda v, xap=xap, ti=ti, cb=cb: v.tensor_scalar(
                xn_tok[ti][:], xap(), sm[:, cb + 2:cb + 3],
                sm[:, cb + 7:cb + 8], ALU.subtract, ALU.mult),
                waits=nw_, chain=True)
            n_done[gt] = r.val("ch_dve")

        wk = w - 1
        if 0 <= wk < NW:
            bik = wk % 2
            ks = wk % 2
            for p in (2 * t, 2 * t + 1):
                kh = p % 2
                kw_ = [("ch_dve", e_done[4 * wk + 3])]
                if wk == 0 and p < 2:
                    kw_.append(("d_w", w0_loads))
                if p >= 2:
                    kw_.append(("ch_dve", ke_done[(wk, p - 2)]))
                r.wait("tensor", kw_)
                for c in range(NCH):
                    r.op("tensor", lambda te, bik=bik, p=p, c=c, kh=kh:
                         te.matmul(
                             ps_k[:, kh * 512:(kh + 1) * 512],
                             wk_sb[:, p * NCH + c, :], xn_win[bik][:, c, :],
                             start=(c == 0), stop=(c == NCH - 1)))
                k_done[(wk, p)] = r.milestone("tensor")
                ekw = [("ch_pe", k_done[(wk, p)])]
                if p == 0 and ks in kstage_dma:
                    ekw.append((f"d_sk{ks}", kstage_dma[ks]))
                r.op("vector", lambda v, p=p, ks=ks, kh=kh: v.tensor_scalar(
                    k_stage[ks][:, p, :], ps_k[:, kh * 512:(kh + 1) * 512],
                    bkp[:, p:p + 1], None, ALU.add), waits=ekw, chain=True)
                ke_done[(wk, p)] = r.val("ch_dve")
            if t == 3:
                r.op("scalar", lambda sc, wk=wk, ks=ks: sc.dma_start(
                    out=scr_k[:, :, wk * 512:(wk + 1) * 512]
                    .rearrange("a p n -> p a n"),
                    in_=k_stage[ks][:]),
                    waits=[("ch_dve", ke_done[(wk, NP - 1)])],
                    incs=[(f"d_sk{ks}", 16)])
                kstage_dma[ks] = r.val(f"d_sk{ks}")

        if 0 <= g2 < NT0:
            evw = [("ch_pe", v_done[g2])]
            if vh2 in vstage_dma:
                evw.append((f"d_sv{vh2}", vstage_dma[vh2]))
            if g2 < 2:
                evw.append(("ch_dve", vinit))
            pv4 = ps_v[:, vh2 * 1024:(vh2 + 1) * 1024].rearrange(
                "p (a h d) -> p a h d", a=NP, h=2)
            vst3 = v_stage[vh2][:].rearrange("p (a y) -> p a y", a=NP)
            r.op("scalar", lambda sc, pv4=pv4, vst3=vst3: sc.copy(
                vst3[:, :, 0:64], pv4[:, :, 0, :]), waits=evw, chain=True)
            r.op("scalar", lambda sc, pv4=pv4, vst3=vst3: sc.copy(
                vst3[:, :, 65:129], pv4[:, :, 1, :]), chain=True)
            ve_done[g2] = r.val("ch_act")
            r.op("scalar", lambda sc, g2=g2, vh2=vh2: sc.dma_start(
                out=scr_v[g2, :, :], in_=v_stage[vh2][:]),
                waits=[("ch_act", ve_done[g2])], incs=[(f"d_sv{vh2}", 16)])
            vstage_dma[vh2] = r.val(f"d_sv{vh2}")

        if t == 3 and gt < NT0 and w + 2 < NW:
            emit_xdma(w + 2)

    own_tpos = {}
    ownld = {}
    xnt_dma = {}
    p0_tail_dve = r.val("ch_dve")
    p0_tail_pe = r.val("ch_pe")
    for qt in range(OWN_QT):
        bi = qt % 2
        lw = ([("ch_dve", ownld[qt - 2])] if qt >= 2
              else [("ch_dve", p0_tail_dve)])
        r.op("sync", lambda sy, qt=qt, bi=bi: sy.dma_start(
            out=x_buf[bi][:, 0, :], in_=xo_in[qt * 128:(qt + 1) * 128, :]),
            waits=lw, incs=[(f"d_x{bi}", 16)])
        xld = r.val(f"d_x{bi}")
        ti = qt % 2
        tw = [(f"d_x{bi}", xld)]
        tw.append(("ch_pe", own_tpos[qt - 2] if qt >= 2 else p0_tail_pe))
        if F_SPARSE and qt >= 2:
            tw.append((f"d_xn{ti}", xnt_dma[ti]))
        ln_tile(lambda bi=bi: x_buf[bi][:, 0, :],
                lambda ti=ti: xn_tok[ti][:], tw, cb=(qt % 2) * 8)
        xn_done = r.val("ch_dve")
        if F_SPARSE:
            r.op("scalar", lambda sc, qt=qt, ti=ti: sc.dma_start(
                out=scr_xnt[qt * 128:(qt + 1) * 128, :], in_=xn_tok[ti][:]),
                waits=[("ch_dve", xn_done)], incs=[(f"d_xn{ti}", 16)])
            xnt_dma[ti] = r.val(f"d_xn{ti}")
        tb = (qt % 2) * 1024
        r.wait("tensor", [("ch_dve", xn_done)])
        for c in range(NCH):
            r.op("tensor", lambda te, ti=ti, c=c, tb=tb: te.transpose(
                ps_t[:, tb + c * 128:tb + (c + 1) * 128],
                xn_tok[ti][:, c * 128:(c + 1) * 128], ident16[:]))
        own_tpos[qt] = r.milestone("tensor")
        r.op("vector", lambda v, qt=qt, tb=tb: v.tensor_copy(
            xn_own[:, :, qt * 128:(qt + 1) * 128],
            ps_t[:, tb:tb + 1024].rearrange("p (c n) -> p c n", c=NCH)),
            waits=[("ch_pe", own_tpos[qt])], chain=True)
        ownld[qt] = r.val("ch_dve")

    xn_own_done = r.val("ch_dve")
    r.wait("tensor", [("ch_dve", xn_own_done)])

    for p in range(NP):
        for qb in range(QB):
            ph = (p * QB + qb) % 2
            if p * QB + qb >= 2:
                r.wait("tensor", [("ch_dve", qevac[p * QB + qb - 2])])
            for c in range(NCH):
                r.op("tensor", lambda te, p=p, qb=qb, c=c, ph=ph: te.matmul(
                    ps_k[:, ph * 512:(ph + 1) * 512],
                    wq_sb[:, p * NCH + c, :],
                    xn_own[:, c, qb * 512:(qb + 1) * 512],
                    start=(c == 0), stop=(c == NCH - 1)))
            qmm = r.milestone("tensor")
            r.op("vector", lambda v, p=p, qb=qb, ph=ph: v.tensor_scalar(
                q_sb[:, p, qb * 512:(qb + 1) * 512],
                ps_k[:, ph * 512:(ph + 1) * 512],
                bqp[:, p:p + 1], None, ALU.add),
                waits=[("ch_pe", qmm)], chain=True)
            if p == 0 and qb == 0:
                qevac = {}
            qevac[p * QB + qb] = r.val("ch_dve")

    for tt in range(OWN_QT):
        gh = tt % 2
        if tt >= 2:
            r.wait("tensor", [("ch_dve", gevac[tt - 2])])
        for c in range(NCH):
            r.op("tensor", lambda te, tt=tt, c=c, gh=gh: te.matmul(
                ps_v[:, gh * 1024:gh * 1024 + NE],
                xn_own[:, c, tt * 128:(tt + 1) * 128],
                wg_sb[:, c, :], start=(c == 0), stop=(c == NCH - 1)))
        gmm = r.milestone("tensor")
        r.op("vector", lambda v, tt=tt, gh=gh: v.tensor_copy(
            scores[:, tt * NE:(tt + 1) * NE],
            ps_v[:, gh * 1024:gh * 1024 + NE]),
            waits=[("ch_pe", gmm)], chain=True)
        if tt == 0:
            gevac = {}
        gevac[tt] = r.val("ch_dve")

    r.wait("sync", [(x, r.val(x)) for x in
                    ("d_x0", "d_x1", "d_sv0", "d_sv1", "d_sk0", "d_sk1",
                     "d_w", "d_xn0", "d_xn1")])
    blocks = [("P0", dict(r.ops))]
    r.ops = {e: [] for e in Rec.ENGS}

    close("x_buf0", "x_buf1", "xn_tok0", "xn_tok1", "xn_win0", "xn_win1",
          "v_stage0", "v_stage1", "k_stage0", "k_stage1", "xsq2_0", "xsq2_1",
          "wq_sb", "wk_sb", "wv_sb", "wg_sb", "ps_t", "ps_v", "ps_k")
    if F_SPARSE:
        close("xn_own")

    o_feat = sb("o_feat", [128, NP, OWNQ], BF16)
    p_buf = sb("p_buf", [128, 4, 2, 512], BF16)
    pm_all = sb("pm_all", [128, NM * 12 * 128], BF16)
    pm0_sb = sb("pm0_sb", [128, Q0KT * 128], F32)
    o_tmp = sb("o_tmp", [128, 512], F32)
    otok2 = sb("otok2", [128, 128], F32)
    p0f = sb("p0f", [128, Q0KT * 128], F32)
    p0t = sb("p0t", [128, Q0KT * 128], BF16)
    if F_SPARSE:
        combine = sb("combine", [128, OWN_QT * NE], F32)
        gm = sb("gm", [128, 4 * NE], F32)
        gmsk1 = sb("gmsk1", [128, OWN_QT * NE], F32)
        gmsk2 = sb("gmsk2", [128, OWN_QT * NE], F32)
        msk_sb = sb("msk_sb", [128, OWN_QT * NE], F32)
        er_sb = sb("er_sb", [128, OWN_QT * 2], F32)
        triL = sb("triL", [128, 128], F32)
        iota8 = sb("iota8", [128, NE], F32)
        iotac = sb("iotac", [128, 1], F32)
        onec32 = sb("onec32", [128, 1], F32)
        oner32 = sb("oner32", [1, 128], F32)
        cnt_row = sb("cnt_row", [1, OWN_QT * NE], F32)
        bases = sb("bases", [1, OWN_QT * NE], F32)
        pg_sb = sb("pg_sb", [128, NE], F32)
        icol = sb("icol", [128, 2, 8], mybir.dt.int32)
        itok = sb("itok", [128, 2 * OWN_QT], mybir.dt.int32)
        ifill = sb("ifill", [128, NE * CAP // 128], mybir.dt.int32)
        ifill2 = sb("ifill2", [128, NE * CAP // 128], mybir.dt.int32)
        zge = sb("zge", [128, D], F32)
    s_stage = [sb(f"s_stage{i}", [128, D], BF16) for i in range(2)]
    v_sb = [sb(f"v_sb{i}", [128, NT, 130], BF16) for i in range(2)]
    k_sb = [sb(f"k_sb{i}", [128, seq], BF16) for i in range(2)]

    PS = psum("PS", [128, 2048], F32)
    PO = [psum(f"PO{i}", [65, 256], F32) for i in range(2)]
    PSmp = psum("PSmp", [128, 512], F32)
    PSdn = psum("PSdn", [128, 512], F32)
    psdn_fence = [0]

    r.op("sync", lambda sy: sy.dma_start(
        out=pm_all[:], in_=pm_in), incs=[("d_pm", 16)])
    r.op("sync", lambda sy: sy.dma_start(out=pm0_sb[:], in_=pm0_in),
         incs=[("d_pm", 16)])
    pm_ld = r.val("d_pm")

    def pmv(mm, slot):
        base = (mm * 12 + slot) * 128
        return pm_all[:, base:base + 128]

    kv_ld = {}
    kv_A = {}
    sv_finals = [r.val("d_sv0"), r.val("d_sv1")]

    def load_pair(p, waits):
        vwaits = list(waits)
        if p < 2:
            vwaits += [("d_sv0", sv_finals[0]), ("d_sv1", sv_finals[1])]
            r.op("sync", lambda sy, p=p: sy.dma_start(
                out=k_sb[p % 2][:, 0:2048], in_=scr_k[p, :, 0:2048]),
                waits=waits, incs=[(f"d_k{p % 2}", 16)])
            kA = r.val(f"d_k{p % 2}")
            r.op("sync", lambda sy, p=p: sy.dma_start(
                out=k_sb[p % 2][:, 2048:seq], in_=scr_k[p, :, 2048:seq]),
                incs=[(f"d_k{p % 2}", 16)])
            r.op("gpsimd", lambda v, p=p: v.dma_start(
                out=v_sb[p % 2][:, 0:16, :],
                in_=scr_v[0:16, :, p * 130:(p + 1) * 130]
                .rearrange("k p y -> p k y")),
                waits=vwaits, incs=[(f"d_v{p % 2}", 16)])
            vA = r.val(f"d_v{p % 2}")
            r.op("gpsimd", lambda v, p=p: v.dma_start(
                out=v_sb[p % 2][:, 16:NT, :],
                in_=scr_v[16:NT, :, p * 130:(p + 1) * 130]
                .rearrange("k p y -> p k y")),
                incs=[(f"d_v{p % 2}", 16)])
            kv_A[p] = (kA, vA)
        else:
            r.op("sync", lambda sy, p=p: sy.dma_start(
                out=k_sb[p % 2][:], in_=scr_k[p, :, :]),
                waits=waits, incs=[(f"d_k{p % 2}", 16)])
            r.op("gpsimd", lambda v, p=p: v.dma_start(
                out=v_sb[p % 2][:],
                in_=scr_v[:, :, p * 130:(p + 1) * 130]
                .rearrange("k p y -> p k y")),
                waits=vwaits, incs=[(f"d_v{p % 2}", 16)])
        kv_ld[p] = (r.val(f"d_k{p % 2}"), r.val(f"d_v{p % 2}"))

    if F_SPARSE:
        r.op("gpsimd", lambda g: g.memset(triL[:], 1.0), chain=True)
        r.op("gpsimd", lambda g: g.affine_select(
            triL[:], triL[:], [[1, 128]], ALU.is_ge, 0.0,
            channel_multiplier=-1, base=-1), chain=True)
        r.op("gpsimd", lambda g: g.iota(
            iota8[:], [[1, NE]], channel_multiplier=0,
            allow_small_or_imprecise_dtypes=True), chain=True)
        r.op("gpsimd", lambda g: g.iota(
            iotac[:], [[1, 1]], channel_multiplier=1,
            allow_small_or_imprecise_dtypes=True), chain=True)
        r.op("gpsimd", lambda g: g.iota(
            itok[:, 0:OWN_QT], [[128, OWN_QT]], channel_multiplier=1),
            chain=True)
        r.op("gpsimd", lambda g: g.tensor_scalar_add(
            itok[:, OWN_QT:2 * OWN_QT], itok[:, 0:OWN_QT], OWNQ),
            chain=True)
        r.op("gpsimd", lambda g: g.memset(ifill[:], 0), chain=True)
        r.op("gpsimd", lambda g: g.memset(ifill2[:], 2 * OWNQ),
             chain=True)
        gp_consts = r.val("ch_pool")
        r.op("vector", lambda v: v.memset(onec32[:], 1.0), chain=True)
        r.op("vector", lambda v: v.memset(oner32[:], 1.0), chain=True)
        r.op("vector", lambda v: v.memset(zge[:], 0.0), chain=True)
        r.op("vector", lambda v: v.memset(bases[0:1, 0:NE], 0.0), chain=True)
        r.op("gpsimd", lambda g: g.dma_start(
            out=scr_gsrc[0:NSLOT, :].rearrange("(t p) o -> p (t o)", p=128),
            in_=ifill[:], allow_slow_non_contiguous=True),
            waits=[("ch_pool", gp_consts)], incs=[("d_ix", 16)])
        r.op("gpsimd", lambda g: g.dma_start(
            out=scr_gdst[0:NSLOT, :].rearrange("(t p) o -> p (t o)", p=128),
            in_=ifill2[:], allow_slow_non_contiguous=True),
            incs=[("d_ix", 16)])
        prefill_ix = r.val("d_ix")

    load_pair(0, [])
    load_pair(1, [])
    if F_SPARSE:
        cast_gate = [("d_k0", kv_ld[0][0]), ("d_k1", kv_ld[1][0]),
                     ("d_pm", pm_ld)]
        for e in range(NE):
            for dst, src in ((scr_w1, w1_in), (scr_w3, w3_in),
                             (scr_w2, w2_in)):
                r.op("gpsimd", lambda g, dst=dst, src=src, e=e: g.dma_start(
                    out=dst[e], in_=src[e]),
                    waits=cast_gate if (e == 0 and dst is scr_w1) else (),
                    incs=[("d_wc", 16)])
        wc_done = r.val("d_wc")
        zv = r.val("ch_dve")
        for zt in range(2 * OWN_QT):
            r.op("sync", lambda sc, zt=zt: sc.dma_start(
                out=scr_ge[zt * 128:(zt + 1) * 128, :], in_=zge[:]),
                waits=[("ch_dve", zv)] if zt == 0 else (),
                incs=[("d_ge", 16)])
        ge_zeroed = r.val("d_ge")

        gv = {}
        gsteps = []
        pg_evac = {}
        ix_after = {}

        def g1a(tt):
            sc_ap = scores[:, tt * NE:(tt + 1) * NE]
            r.op("vector", lambda v, sc=sc_ap: v.reduce_max(
                sm[:, 0:1], sc, axis=AX.X),
                waits=[("ch_act", r.val("ch_act")),
                       ("ch_pool", gp_consts)], chain=True)
            r.op("vector", lambda v, sc=sc_ap, tt=tt: v.tensor_scalar(
                gmsk1[:, tt * NE:(tt + 1) * NE], sc, sm[:, 0:1], None,
                ALU.is_equal), chain=True)
            r.op("vector", lambda v, sc=sc_ap, tt=tt: v.scalar_tensor_tensor(
                gm[:, NE:2 * NE], gmsk1[:, tt * NE:(tt + 1) * NE], -1e9, sc,
                ALU.mult, ALU.add), chain=True)
            r.op("vector", lambda v: v.reduce_max(
                sm[:, 1:2], gm[:, NE:2 * NE], axis=AX.X), chain=True)
            r.op("vector", lambda v, tt=tt: v.tensor_scalar(
                gmsk2[:, tt * NE:(tt + 1) * NE], gm[:, NE:2 * NE],
                sm[:, 1:2], None, ALU.is_equal), chain=True)
            r.op("vector", lambda v: v.tensor_sub(
                sm[:, 2:3], sm[:, 0:1], sm[:, 1:2]), chain=True)
            dcol = r.val("ch_dve")
            r.op("scalar", lambda scl: scl.activation(
                sm[:, 3:4], sm[:, 2:3], AF.Sigmoid),
                waits=[("ch_dve", dcol)], chain=True)
            r.op("scalar", lambda scl: scl.activation(
                sm[:, 4:5], sm[:, 2:3], AF.Sigmoid, scale=-1.0), chain=True)
            gv[("sig", tt)] = r.val("ch_act")

        def g1b(tt):
            r.op("vector", lambda v, tt=tt: v.tensor_scalar(
                gm[:, 3 * NE:4 * NE], gmsk1[:, tt * NE:(tt + 1) * NE],
                sm[:, 3:4], None, ALU.mult),
                waits=[("ch_act", gv[("sig", tt)])], chain=True)
            r.op("vector", lambda v, tt=tt: v.scalar_tensor_tensor(
                combine[:, tt * NE:(tt + 1) * NE],
                gmsk2[:, tt * NE:(tt + 1) * NE],
                sm[:, 4:5], gm[:, 3 * NE:4 * NE], ALU.mult, ALU.add),
                chain=True)
            r.op("vector", lambda v, tt=tt: v.tensor_add(
                msk_sb[:, tt * NE:(tt + 1) * NE],
                gmsk1[:, tt * NE:(tt + 1) * NE],
                gmsk2[:, tt * NE:(tt + 1) * NE]), chain=True)
            r.op("vector", lambda v, tt=tt: v.tensor_mul(
                gm[:, NE:2 * NE], gmsk1[:, tt * NE:(tt + 1) * NE],
                iota8[:]), chain=True)
            r.op("vector", lambda v, tt=tt: v.reduce_sum(
                er_sb[:, tt * 2:tt * 2 + 1], gm[:, NE:2 * NE], axis=AX.X),
                chain=True)
            r.op("vector", lambda v, tt=tt: v.tensor_mul(
                gm[:, NE:2 * NE], gmsk2[:, tt * NE:(tt + 1) * NE],
                iota8[:]), chain=True)
            r.op("vector", lambda v, tt=tt: v.reduce_sum(
                er_sb[:, tt * 2 + 1:tt * 2 + 2], gm[:, NE:2 * NE],
                axis=AX.X), chain=True)

        def gcmb():
            r.op("gpsimd", lambda sc: sc.dma_start(
                out=scr_cmb.rearrange("(t p) c -> p t c", p=128),
                in_=combine[:].rearrange("p (t c) -> p t c", c=NE)),
                waits=[("ch_dve", r.val("ch_dve"))], incs=[("d_cm", 16)])

        def gcnt():
            r.wait("tensor", [("ch_dve", max(r.val("ch_dve"),
                                             psdn_fence[0])),
                              ("ch_pool", gp_consts)])
            for tt in range(OWN_QT):
                r.op("tensor", lambda te, tt=tt: te.matmul(
                    PSdn[0:1, 384 + tt * NE:384 + (tt + 1) * NE],
                    onec32[:], msk_sb[:, tt * NE:(tt + 1) * NE],
                    start=True, stop=True))
            cnt_mm = r.milestone("tensor")
            r.op("vector", lambda v: v.tensor_copy(
                cnt_row[0:1, :], PSdn[0:1, 384:384 + OWN_QT * NE]),
                waits=[("ch_pe", cnt_mm)], chain=True)
            psdn_fence[0] = r.val("ch_dve")

        def gbase():
            for tt in range(1, OWN_QT):
                r.op("vector", lambda v, tt=tt: v.tensor_add(
                    bases[0:1, tt * NE:(tt + 1) * NE],
                    bases[0:1, (tt - 1) * NE:tt * NE],
                    cnt_row[0:1, (tt - 1) * NE:tt * NE]), chain=True)
            gv["bases"] = r.val("ch_dve")

        def g3a(tt):
            quad = tt % 2
            qb = 384 + quad * NE
            pw = pg_evac[tt - 2] if tt >= 2 else gv["bases"]
            r.wait("tensor", [("ch_dve", max(pw, psdn_fence[0]))])
            r.op("tensor", lambda te, tt=tt, qb=qb: te.matmul(
                PSdn[:, qb:qb + NE],
                triL[:], msk_sb[:, tt * NE:(tt + 1) * NE],
                start=True, stop=False))
            r.op("tensor", lambda te, tt=tt, qb=qb: te.matmul(
                PSdn[:, qb:qb + NE],
                oner32[:], bases[0:1, tt * NE:(tt + 1) * NE],
                start=False, stop=True))
            pmm = r.milestone("tensor")
            r.op("vector", lambda v, qb=qb: v.tensor_copy(
                pg_sb[:], PSdn[:, qb:qb + NE]),
                waits=[("ch_pe", pmm)], chain=True)
            pg_evac[tt] = r.val("ch_dve")
            psdn_fence[0] = pg_evac[tt]

        def g3b(tt):
            qq = tt % 2
            iw = ([(f"d_ix{i}", ix_after[tt - 2][i]) for i in range(4)]
                  if tt >= 2 else [])
            first = [True]
            for rk in range(2):
                gmk = gmsk1 if rk == 0 else gmsk2
                r.op("vector", lambda v, tt=tt, gmk=gmk: v.tensor_mul(
                    gm[:, 0:NE], pg_sb[:], gmk[:, tt * NE:(tt + 1) * NE]),
                    waits=iw if first[0] else (), chain=True)
                first[0] = False
                r.op("vector", lambda v: v.reduce_sum(
                    sm[:, 2:3], gm[:, 0:NE], axis=AX.X), chain=True)
                r.op("vector", lambda v, tt=tt, rk=rk: v.scalar_tensor_tensor(
                    sm[:, 3:4], er_sb[:, tt * 2 + rk:tt * 2 + rk + 1],
                    float(CAP), sm[:, 2:3], ALU.mult, ALU.add), chain=True)
                r.op("vector", lambda v: v.tensor_scalar(
                    sm[:, 4:5], sm[:, 2:3], float(CAP), None, ALU.is_ge),
                    chain=True)
                r.op("vector", lambda v: v.scalar_tensor_tensor(
                    sm[:, 5:6], sm[:, 4:5], float(NSLOT + 8192), sm[:, 3:4],
                    ALU.mult, ALU.add), chain=True)
                r.op("vector", lambda v, rk=rk, qq=qq: v.tensor_copy(
                    icol[:, qq, rk:rk + 1], sm[:, 5:6]), chain=True)
            pgd = r.val("ch_dve")
            for rk in range(2):
                sA, sB = f"d_ix{2 * rk}", f"d_ix{2 * rk + 1}"
                wA = [("ch_dve", pgd), (sA, r.val(sA))]
                wB = [("ch_dve", pgd), (sB, r.val(sB))]
                if tt == 0:
                    wA.append(("d_ix", prefill_ix))
                    wB.append(("d_ix", prefill_ix))
                r.op("gpsimd", lambda g, rk=rk, qq=qq, tt=tt:
                     g.indirect_dma_start(
                         out=scr_gsrc,
                         out_offset=bass.IndirectOffsetOnAxis(
                             icol[:, qq, rk:rk + 1], 0),
                         in_=itok[:, tt:tt + 1], in_offset=None),
                     waits=wA, incs=[(sA, 16)])
                r.op("gpsimd", lambda g, rk=rk, qq=qq, tt=tt:
                     g.indirect_dma_start(
                         out=scr_gdst,
                         out_offset=bass.IndirectOffsetOnAxis(
                             icol[:, qq, rk:rk + 1], 0),
                         in_=itok[:, rk * OWN_QT + tt:rk * OWN_QT + tt + 1],
                         in_offset=None),
                     waits=wB, incs=[(sB, 16)])
            ix_after[tt] = tuple(r.val(f"d_ix{i}") for i in range(4))

        def grdbk():
            ix_done = [(f"d_ix{i}", r.val(f"d_ix{i}")) for i in range(4)]
            r.op("sync", lambda sy: sy.dma_start(
                out=gsrc_sb[:],
                in_=scr_gsrc[0:NE * CAP, :]
                .rearrange("(t p) o -> p (t o)", p=128),
                allow_slow_non_contiguous=True),
                waits=ix_done, incs=[("d_gi", 16)])
            r.op("sync", lambda sy: sy.dma_start(
                out=gdst_sb[:],
                in_=scr_gdst[0:NE * CAP, :]
                .rearrange("(t p) o -> p (t o)", p=128),
                allow_slow_non_contiguous=True), incs=[("d_gi", 16)])

        for tt in range(OWN_QT):
            gsteps.append(lambda tt=tt: g1a(tt))
            gsteps.append(lambda tt=tt: g1b(tt))
        gsteps.append(gcmb)
        gsteps.append(gcnt)
        gsteps.append(gbase)
        for tt in range(OWN_QT):
            gsteps.append(lambda tt=tt: g3a(tt))
            gsteps.append(lambda tt=tt: g3b(tt))
        gsteps.append(grdbk)
    else:
        gsteps = []
    gq = [0]

    pair_pe_done = {}
    bias_ready = {}
    exp_done = {}
    pv_done = {}
    Gctr = [0]
    pending = []
    pvq = []
    po_free = [0]
    bias_prev_dve = [0]

    def sample_steps(p, m):
        nkt = 8 * m + 8
        bias_c = 16 + 2 * (m % 2)
        kb = k_sb[p % 2]
        sv = {}

        def st0():
            w0 = [("ch_dve", sv.get("prev_bias", bias_prev_dve[0]))]
            if p < 2 and m >= 2:
                w0.append((f"d_k{p % 2}", kv_ld[p][0]))
            r.wait("tensor", w0)
            for hh in range(2):
                h0 = hh * 64
                kslice = kb[h0:h0 + 64, 0:nkt * 128].rearrange(
                    "p (n s) -> p s n", s=nkt)[:, 0, :]
                r.op("tensor", lambda te, kslice=kslice, h0=h0, p=p, m=m,
                     hh=hh: te.matmul(
                         PSmp[:, hh * 256:hh * 256 + 256], kslice,
                         q_sb[h0:h0 + 64, p, m * 256:(m + 1) * 256],
                         start=True, stop=True, tile_position=(h0, 0)))
                sv["mm%d" % hh] = r.milestone("tensor")
                if hh == 0:
                    r.wait("tensor", [("ch_pe", sv["mm0"])])

        def st1():
            for hh in range(2):
                r.op("vector", lambda v, hh=hh: v.reduce_max(
                    sm[:, 8 + hh:9 + hh], PSmp[:, hh * 256:hh * 256 + 256],
                    axis=AX.X),
                    waits=[("ch_pe", sv["mm1"])] if hh == 0 else (),
                    chain=True)
            sv["red"] = r.val("ch_dve")

        def st2():
            r.wait("tensor", [("ch_dve", sv["red"])])
            for hh in range(2):
                r.op("tensor", lambda te, hh=hh: te.transpose(
                    PSmp[0:1, hh * 256:hh * 256 + 128],
                    sm[:, 8 + hh:9 + hh], ident32[:]))
            sv["tr"] = r.milestone("tensor")

        def st3():
            for hh in range(2):
                r.op("vector", lambda v, hh=hh: v.reduce_max(
                    sm[0:1, 12 + hh:13 + hh],
                    PSmp[0:1, hh * 256:hh * 256 + 128], axis=AX.X),
                    waits=[("ch_pe", sv["tr"])] if hh == 0 else (),
                    chain=True)
            r.op("vector", lambda v: v.tensor_max(
                sm[0:1, 14:15], sm[0:1, 12:13], sm[0:1, 13:14]), chain=True)
            r.op("vector", lambda v: v.tensor_copy(
                sm16[0:1, 0:1], sm[0:1, 14:15]), chain=True)
            sv["cast"] = r.val("ch_dve")

        def st4():
            r.wait("tensor", [("ch_dve", sv["cast"])])
            r.op("tensor", lambda te: te.matmul(
                PSmp[:, 384:385], ones16[0:1, :], sm16[0:1, 0:1],
                start=True, stop=True))
            sv["bc"] = r.milestone("tensor")

        def st5():
            r.op("vector", lambda v: v.tensor_scalar(
                sm[:, bias_c:bias_c + 1], PSmp[:, 384:385],
                -SCALE, None, ALU.mult),
                waits=[("ch_pe", sv["bc"])], chain=True)
            bias_ready[(p, m)] = r.val("ch_dve")
            bias_prev_dve[0] = bias_ready[(p, m)]

        return [st0, st1, st2, st3, st4, st5]

    def dance_steps(p, m, po_evac_val, lastG=None):
        sv = {"ev": po_evac_val}

        def d0():
            while lastG not in pv_done and pvq:
                pvq.pop(0)()
            for hh in range(2):
                r.op("vector", lambda v, hh=hh: v.tensor_copy(
                    o_tmp[0:65, hh * 256:(hh + 1) * 256], PO[hh][:]),
                    waits=[("ch_pe", pv_done[lastG])] if hh == 0 else (),
                    chain=True)
            po_free[0] = r.val("ch_dve")
            sv["ev"] = po_free[0]

        def d1():
            r.wait("tensor", [("ch_dve", max(sv["ev"], psdn_fence[0]))])
            for hh in range(2):
                r.op("tensor", lambda te, hh=hh: te.transpose(
                    PSdn[:, hh * 65:(hh + 1) * 65],
                    o_tmp[0:65, hh * 256:hh * 256 + 128],
                    ident32[0:65, 0:65]))
            sv["t1"] = r.milestone("tensor")

        def d2():
            first = [("ch_pe", sv["t1"])]
            for hh in range(2):
                r.op("vector", lambda v, hh=hh: v.reciprocal(
                    sm[:, 20:21], PSdn[:, hh * 65 + 64:hh * 65 + 65]),
                    waits=first if hh == 0 else (), chain=True)
                r.op("vector", lambda v, hh=hh: v.tensor_scalar(
                    otok2[:, hh * 64:(hh + 1) * 64],
                    PSdn[:, hh * 65:hh * 65 + 64], sm[:, 20:21],
                    None, ALU.mult), chain=True)
            sv["n1"] = r.val("ch_dve")
            psdn_fence[0] = sv["n1"]

        def d3():
            r.wait("tensor", [("ch_dve", max(sv["n1"], psdn_fence[0]))])
            r.op("tensor", lambda te: te.transpose(
                PSdn[:, 256:384], otok2[:], ident32[:]))
            for hh in range(2):
                r.op("tensor", lambda te, hh=hh: te.transpose(
                    PSdn[:, hh * 65:(hh + 1) * 65],
                    o_tmp[0:65, hh * 256 + 128:hh * 256 + 256],
                    ident32[0:65, 0:65]))
            sv["t2"] = r.milestone("tensor")

        def d4():
            r.op("vector", lambda v, p=p, m=m: v.tensor_scalar(
                o_feat[:, p, m * 256:m * 256 + 128],
                PSdn[:, 256:384], bvt[:, p:p + 1], None, ALU.add),
                waits=[("ch_pe", sv["t2"])], chain=True)
            for hh in range(2):
                r.op("vector", lambda v, hh=hh: v.reciprocal(
                    sm[:, 20:21], PSdn[:, hh * 65 + 64:hh * 65 + 65]),
                    chain=True)
                r.op("vector", lambda v, hh=hh: v.tensor_scalar(
                    otok2[:, hh * 64:(hh + 1) * 64],
                    PSdn[:, hh * 65:hh * 65 + 64], sm[:, 20:21],
                    None, ALU.mult), chain=True)
            sv["n2"] = r.val("ch_dve")
            psdn_fence[0] = sv["n2"]

        def d5():
            r.wait("tensor", [("ch_dve", max(sv["n2"], psdn_fence[0]))])
            r.op("tensor", lambda te: te.transpose(
                PSdn[:, 256:384], otok2[:], ident32[:]))
            sv["t3"] = r.milestone("tensor")

        def d6():
            r.op("vector", lambda v, p=p, m=m: v.tensor_scalar(
                o_feat[:, p, m * 256 + 128:m * 256 + 256],
                PSdn[:, 256:384], bvt[:, p:p + 1], None, ALU.add),
                waits=[("ch_pe", sv["t3"])], chain=True)
            psdn_fence[0] = r.val("ch_dve")

        if po_evac_val is None:
            return [d0, d1, d2, d3, d4, d5, d6]
        return [d1, d2, d3, d4, d5, d6]

    for p in range(NP):
        hd = kv_A[p] if p < 2 else kv_ld[p]
        pw = [(f"d_k{p % 2}", hd[0]), (f"d_v{p % 2}", hd[1]),
              ("d_pm", pm_ld)]
        r.wait("tensor", pw)

        for m in range(NM):
            if p < 2 and m == 2:
                r.wait("tensor", [(f"d_k{p % 2}", kv_ld[p][0]),
                                  (f"d_v{p % 2}", kv_ld[p][1])])
            nkt = 8 * m + 8
            ngr = nkt // 2
            bias_c = 16 + 2 * (m % 2)

            if (p, m) not in bias_ready:
                if F_SAMPLE:
                    for st in sample_steps(p, m):
                        st()
                else:
                    r.op("vector", lambda v, bias_c=bias_c: v.memset(
                        sm[:, bias_c:bias_c + 2], -12.0), chain=True)
                    bias_ready[(p, m)] = r.val("ch_dve")

            smp = sample_steps(p, m + 1) if m + 1 < NM else []
            steps = smp[:2] + pending + smp[2:]
            pending = []
            si = 0

            for g in range(ngr):
                G = Gctr[0]
                Gctr[0] += 1
                kts = (2 * g, 2 * g + 1)
                quad0 = (2 * g) % 4
                par = G % 4
                sw = []
                if G - 2 in exp_done:
                    sw.append(("ch_act", exp_done[G - 2]))
                if g == 0:
                    sw.append(("ch_dve", bias_ready[(p, m)]))
                r.wait("tensor", sw)
                for kt in kts:
                    for hh in range(2):
                        h0 = hh * 64
                        r.op("tensor", lambda te, p=p, m=m, kt=kt, hh=hh,
                             h0=h0: te.matmul(
                                 PS[:, hh * 1024 + (kt % 4) * 256:
                                    hh * 1024 + (kt % 4) * 256 + 256],
                                 k_sb[p % 2][h0:h0 + 64,
                                             kt * 128:(kt + 1) * 128],
                                 q_sb[h0:h0 + 64, p, m * 256:(m + 1) * 256],
                                 start=True, stop=True,
                                 tile_position=(h0, 0)))
                smm = r.milestone("tensor")
                ew = [("ch_pe", smm)]
                if G - 4 in pv_done:
                    ew.append(("ch_pe", pv_done[G - 4]))
                ps_in = PS[:].rearrange("p (h n) -> p h n", h=2)[
                    :, :, quad0 * 256:quad0 * 256 + 512]
                r.op("scalar", lambda s, par=par, quad0=quad0, ps_in=ps_in,
                     bias_c=bias_c: s.activation(
                         p_buf[:, par, :, :], ps_in, AF.Exp,
                         bias=sm[:, bias_c:bias_c + 1], scale=SCALE),
                     waits=ew, chain=True)
                exp_done[G] = r.val("ch_act")
                nmask = 0
                for kt in (kts if F_MASK else ()):
                    slots = []
                    i1 = kt - 8 * m
                    if 0 <= i1 < 8:
                        slots.append((0, i1))
                    i2 = kt - (8 * m + 4)
                    if 0 <= i2 < 4:
                        slots.append((1, 8 + i2))
                    for sub, slot in slots:
                        for hh in range(2):
                            col = (kt % 2) * 256 + sub * 128
                            r.op("vector", lambda v, par=par, hh=hh, col=col,
                                 m=m, slot=slot: v.tensor_mul(
                                     p_buf[:, par, hh, col:col + 128],
                                     p_buf[:, par, hh, col:col + 128],
                                     pmv(m, slot)),
                                 waits=[("ch_act", exp_done[G])], chain=True)
                            nmask += 1
                vw = [("ch_act", exp_done[G])]
                if nmask:
                    vw.append(("ch_dve", r.val("ch_dve")))

                lastkt = -1 if (m == 0 and F_EXACT) else nkt - 1

                def emit_pv(vw=vw, kts=kts, par=par, p=p, lastkt=lastkt,
                            G=G, g0=(g == 0)):
                    if g0:
                        vw = vw + [("ch_dve", po_free[0])]
                    r.wait("tensor", vw)
                    for kt in kts:
                        for hh in range(2):
                            r.op("tensor", lambda te, p=p, kt=kt, hh=hh,
                                 par=par: te.matmul(
                                     PO[hh][:],
                                     v_sb[p % 2][:, kt,
                                                 hh * 65:(hh + 1) * 65],
                                     p_buf[:, par, hh,
                                           (kt % 2) * 256:
                                           (kt % 2) * 256 + 256],
                                     start=(kt == 0), stop=(kt == lastkt)))
                    pv_done[G] = r.milestone("tensor")

                pvq.append(emit_pv)
                if len(pvq) > 3:
                    pvq.pop(0)()
                if si < len(steps):
                    steps[si]()
                    si += 1
                elif gq[0] < len(gsteps) and G % 2 == 0:
                    gsteps[gq[0]]()
                    gq[0] += 1
            defer_po = F_DANCE and 0 < m < NM - 1
            if not defer_po:
                while pvq:
                    pvq.pop(0)()
            while si < len(steps):
                steps[si]()
                si += 1

            if m == 0 and F_EXACT:
                r.wait("tensor", [("ch_act", r.val("ch_act"))])
                for hh in range(2):
                    h0 = hh * 64
                    r.op("tensor", lambda te, p=p, hh=hh, h0=h0: te.matmul(
                        PS[:, hh * 1024:hh * 1024 + Q0KT * 128],
                        q_sb[h0:h0 + 64, p, 0:128],
                        k_sb[p % 2][h0:h0 + 64, 0:Q0KT * 128],
                        start=True, stop=True, tile_position=(h0, 0)))
                q0mm = r.milestone("tensor")
                for hh in range(2):
                    r.op("vector", lambda v, hh=hh: v.tensor_add(
                        PS[:, hh * 1024:hh * 1024 + Q0KT * 128],
                        PS[:, hh * 1024:hh * 1024 + Q0KT * 128],
                        pm0_sb[:]),
                        waits=[("ch_pe", q0mm)], chain=True)
                    r.op("vector", lambda v, hh=hh: v.reduce_max(
                        sm[:, 8 + hh:9 + hh],
                        PS[:, hh * 1024:hh * 1024 + Q0KT * 128],
                        axis=AX.X), chain=True)
                    r.op("vector", lambda v, hh=hh: v.tensor_scalar(
                        sm[:, 10 + hh:11 + hh], sm[:, 8 + hh:9 + hh],
                        -SCALE, None, ALU.mult), chain=True)
                q0bias = r.val("ch_dve")
                for hh in range(2):
                    r.op("scalar", lambda s, hh=hh: s.activation(
                        p0f[:], PS[:, hh * 1024:hh * 1024 + Q0KT * 128],
                        AF.Exp, bias=sm[:, 10 + hh:11 + hh], scale=SCALE),
                        waits=[("ch_dve", q0bias),
                               ("ch_pe", r.val("ch_pe"))], chain=True)
                    e0 = r.val("ch_act")
                    r.wait("tensor", [("ch_act", e0)])
                    for kt in range(Q0KT):
                        r.op("tensor", lambda te, kt=kt, hh=hh: te.transpose(
                            PS[:, hh * 1024 + 512 + kt * 128:
                               hh * 1024 + 512 + (kt + 1) * 128],
                            p0f[:, kt * 128:(kt + 1) * 128], ident32[:]))
                    t0 = r.milestone("tensor")
                    r.op("vector", lambda v, hh=hh: v.tensor_copy(
                        p0t[:], PS[:, hh * 1024 + 512:hh * 1024 + 1024]),
                        waits=[("ch_pe", t0)], chain=True)
                    c0 = r.val("ch_dve")
                    r.wait("tensor", [("ch_dve", c0)])
                    for kt in range(Q0KT):
                        r.op("tensor", lambda te, p=p, kt=kt, hh=hh:
                             te.matmul(
                                 PO[hh][:, 0:128],
                                 v_sb[p % 2][:, kt, hh * 65:(hh + 1) * 65],
                                 p0t[:, kt * 128:(kt + 1) * 128],
                                 start=False, stop=(kt == Q0KT - 1)))
                    r.milestone("tensor")

            if defer_po:
                if F_DANCE:
                    pending = pending + dance_steps(p, m, None,
                                                    Gctr[0] - 1)
            else:
                pvd = r.val("ch_pe")
                for hh in range(2):
                    r.op("vector", lambda v, hh=hh: v.tensor_copy(
                        o_tmp[0:65, hh * 256:(hh + 1) * 256], PO[hh][:]),
                        waits=[("ch_pe", pvd)], chain=True)
                po_free[0] = r.val("ch_dve")
                if F_DANCE:
                    pending = pending + dance_steps(p, m, po_free[0])

        pair_pe_done[p] = r.milestone("tensor")
        if p + 2 < NP:
            load_pair(p + 2, [("ch_pe", pair_pe_done[p])])

    close("k_sb0", "k_sb1")
    wo_sb = sb("wo_sb", [128, NCH, D], BF16)
    r.op("gpsimd", lambda g: g.dma_start(
        out=wo_sb[:], in_=wo_in),
        waits=[("ch_pe", pair_pe_done[NP - 1])], incs=[("d_w", 16)])
    wo_ld = r.val("d_w")

    while pvq:
        pvq.pop(0)()
    for st in pending:
        st()
    pending = []
    while gq[0] < len(gsteps):
        gsteps[gq[0]]()
        gq[0] += 1

    sdma = {}
    r.wait("tensor", [("d_w", wo_ld), ("ch_dve", r.val("ch_dve")),
                      ("ch_act", r.val("ch_act"))])
    for qt in range(OWN_QT):
        sh = qt % 2
        for mh in range(2):
            quad = (qt * 2 + mh) % 4
            if qt * 2 + mh >= 4:
                r.wait("tensor", [("ch_dve", sevac[qt * 2 + mh - 4])])
            for p in range(NP):
                r.op("tensor", lambda te, qt=qt, mh=mh, p=p, quad=quad:
                     te.matmul(
                         PS[:, quad * 512:(quad + 1) * 512],
                         o_feat[:, p, qt * 128:(qt + 1) * 128],
                         wo_sb[:, p, mh * 512:(mh + 1) * 512],
                         start=(p == 0), stop=False))
            r.op("tensor", lambda te, mh=mh, quad=quad: te.matmul(
                PS[:, quad * 512:(quad + 1) * 512], ones16[0:1, :],
                bo_sb[0:1, mh * 512:(mh + 1) * 512],
                start=False, stop=True))
            omm = r.milestone("tensor")
            evw = [("ch_pe", omm)]
            if mh == 0 and sh in sdma:
                evw.append((f"d_ss{sh}", sdma[sh]))
            r.op("vector", lambda v, sh=sh, mh=mh, quad=quad: v.tensor_copy(
                s_stage[sh][:, mh * 512:(mh + 1) * 512],
                PS[:, quad * 512:(quad + 1) * 512]), waits=evw, chain=True)
            if qt == 0 and mh == 0:
                sevac = {}
            sevac[qt * 2 + mh] = r.val("ch_dve")
        r.op("scalar", lambda s, qt=qt, sh=sh: s.dma_start(
            out=scr_s[qt, :, :], in_=s_stage[sh][:]),
            waits=[("ch_dve", sevac[qt * 2 + 1])], incs=[(f"d_ss{sh}", 16)])
        sdma[sh] = r.val(f"d_ss{sh}")

    r.wait("sync", [(x, r.val(x)) for x in
                    ("d_k0", "d_k1", "d_v0", "d_v1", "d_pm", "d_ss0",
                     "d_ss1", "d_w", "d_ix", "d_ix0", "d_ix1", "d_ix2",
                     "d_ix3", "d_ge", "d_cm", "d_gi")])
    blocks.append(("P2", dict(r.ops)))
    r.ops = {e: [] for e in Rec.ENGS}

    close("wo_sb", "v_sb0", "v_sb1", "p_buf", "pm_all", "pm0_sb",
          "o_tmp", "otok2", "p0f", "p0t", "s_stage0", "s_stage1",
          "o_feat", "PS", "PO0", "PO1", "PSmp", "PSdn", "q_sb",
          "combine", "gm", "gmsk1", "gmsk2", "msk_sb", "er_sb", "triL",
          "iota8", "iotac", "onec32", "oner32", "cnt_row", "bases",
          "pg_sb", "icol", "itok", "ifill", "ifill2", "zge")

    if F_SPARSE:
        xe_tok = [sb(f"xe_tok{i}", [128, CT, D], BF16) for i in range(2)]
        cmb_g = [sb(f"cmb_g{i}", [128, CT, NE], F32) for i in range(2)]
        xe_feat = [sb(f"xe_feat{i}", [128, NCH, CAP], BF16) for i in range(2)]
        h_buf = sb("h_buf", [128, HID // 128, CAP], BF16)
        wsl = [sb(f"wsl{i}", [128, NCH, 1024], BF16) for i in range(3)]
        ge_stage = [sb(f"ge_stage{i}", [128, D], F32) for i in range(2)]

        pa = psum("pa", [128, 2048], F32)
        gps = psum("gps", [128, 1024], F32)
        pst3 = psum("pst3", [128, 1024], BF16)

        gi_ld = r.val("d_gi")
        for i in range(2):
            r.op("vector", lambda v, i=i: v.memset(
                xe_tok[i][:].rearrange("p t d -> p (t d)"), 0.0), chain=True)
            r.op("vector", lambda v, i=i: v.memset(
                cmb_g[i][:].rearrange("p t c -> p (t c)"), 0.0), chain=True)
        xz = r.val("ch_dve")

        xe_gath = {}
        tr_done = {}
        fe_done = {}

        def gather_e(e, waits):
            par = e % 2
            w = list(waits) + [("d_gi", gi_ld)]
            for st in range(CT):
                col = e * CT + st
                r.op("gpsimd", lambda g, par=par, st=st, col=col:
                     g.indirect_dma_start(
                         out=xe_tok[par][:, st, :], out_offset=None,
                         in_=scr_xnt,
                         in_offset=bass.IndirectOffsetOnAxis(
                             gsrc_sb[:, col:col + 1], 0),
                         ),
                     waits=w if st == 0 else (),
                     incs=[(f"d_xe{par}", 16)])
                r.op("gpsimd", lambda g, par=par, st=st, col=col:
                     g.indirect_dma_start(
                         out=cmb_g[par][:, st, :], out_offset=None,
                         in_=scr_cmb,
                         in_offset=bass.IndirectOffsetOnAxis(
                             gsrc_sb[:, col:col + 1], 0),
                         ),
                     incs=[(f"d_xe{par}", 16)])
            xe_gath[e] = r.val(f"d_xe{par}")

        def transpose_e(e):
            par = e % 2
            r.wait("tensor", [(f"d_xe{par}", xe_gath[e]),
                              ("ch_dve", fe_done.get(e - 2, xz))])
            for st in range(CT):
                if st >= 1:
                    r.wait("tensor", [("ch_dve", r.val("ch_dve"))])
                for c in range(NCH):
                    r.op("tensor", lambda te, par=par, st=st, c=c:
                         te.transpose(
                             pst3[:, c * 128:(c + 1) * 128],
                             xe_tok[par][:, st, c * 128:(c + 1) * 128],
                             ident16[:]))
                ms = r.milestone("tensor")
                r.op("vector", lambda v, par=par, st=st: v.tensor_copy(
                    xe_feat[par][:, :, st * 128:(st + 1) * 128],
                    pst3[:].rearrange("p (c n) -> p c n", c=NCH)),
                    waits=[("ch_pe", ms)], chain=True)
            tr_done[e] = r.val("ch_pe")
            fe_done[e] = r.val("ch_dve")

        hsil_done = {}
        hmul_done = {}
        wsl_dma = [0, 0, 0]
        slot_free = [0, 0, 0]
        gb_dma = {}

        def wl(slot, src_ap):
            r.op("sync", lambda sy, slot=slot, src_ap=src_ap: sy.dma_start(
                out=wsl[slot][:], in_=src_ap),
                waits=[("ch_pe", slot_free[slot])],
                incs=[(f"d_ws{slot}", 16)])
            wsl_dma[slot] = r.val(f"d_ws{slot}")

        def h_phase_e(e, wh, kind, slot):
            par0 = e % 2
            r.wait("tensor", [(f"d_ws{slot}", wsl_dma[slot]),
                              ("ch_dve", fe_done[e])])
            for hc8 in range(NCH):
                hc = wh * NCH + hc8
                for gi, (t0, t1) in enumerate(ETG):
                    par = (hc8 * len(ETG) + gi) % 2
                    r.wait("tensor", [("ch_act", hsil_done.get(par, 0)),
                                      ("ch_dve", hmul_done.get(par, 0))])
                    for c in range(NCH):
                        r.op("tensor", lambda te, c=c, hc8=hc8, t0=t0, t1=t1,
                             par=par, slot=slot, par0=par0: te.matmul(
                                 pa[:, par * 1024:par * 1024 + (t1 - t0)],
                                 wsl[slot][:, c, hc8 * 128:(hc8 + 1) * 128],
                                 xe_feat[par0][:, c, t0:t1],
                                 start=(c == 0), stop=(c == NCH - 1)))
                    hmm = r.milestone("tensor")
                    if kind == 1:
                        r.op("scalar", lambda s, hc=hc, t0=t0, t1=t1, par=par:
                             s.activation(
                                 h_buf[:, hc, t0:t1],
                                 pa[:, par * 1024:par * 1024 + (t1 - t0)],
                                 AF.Silu),
                             waits=[("ch_pe", hmm)], chain=True)
                        hsil_done[par] = r.val("ch_act")
                    else:
                        r.op("vector", lambda v, hc=hc, t0=t0, t1=t1, par=par:
                             v.tensor_tensor(
                                 h_buf[:, hc, t0:t1], h_buf[:, hc, t0:t1],
                                 pa[:, par * 1024:par * 1024 + (t1 - t0)],
                                 ALU.mult),
                             waits=[("ch_pe", hmm)], chain=True)
                        hmul_done[par] = r.val("ch_dve")
            slot_free[slot] = r.milestone("tensor")

        def g_phase_e(e, s1, s2):
            par0 = e % 2
            hdone = r.val("ch_dve")
            r.wait("tensor", [(f"d_ws{s1}", wsl_dma[s1]),
                              (f"d_ws{s2}", wsl_dma[s2]),
                              ("ch_dve", hdone)])
            for st in range(CT):
                sh = st % 2
                for mh in range(2):
                    gq = (st * 2 + mh) % 2
                    gw = [("ch_dve", gacc_done.get(gq, 0))]
                    if mh == 0 and sh in gb_dma:
                        gw.append(("d_gb", gb_dma[sh]))
                    r.wait("tensor", gw)
                    for hc in range(HID // 128):
                        slot = s1 if hc < NCH else s2
                        r.op("tensor", lambda te, st=st, mh=mh, hc=hc,
                             gq=gq, slot=slot: te.matmul(
                                 gps[:, gq * 512:(gq + 1) * 512],
                                 h_buf[:, hc, st * 128:(st + 1) * 128],
                                 wsl[slot][:, hc % NCH,
                                           mh * 512:(mh + 1) * 512],
                                 start=(hc == 0),
                                 stop=(hc == HID // 128 - 1)))
                    gmm = r.milestone("tensor")
                    r.op("vector", lambda v, st=st, mh=mh, e=e, gq=gq,
                         sh=sh, par0=par0: v.tensor_scalar(
                             ge_stage[sh][:, mh * 512:(mh + 1) * 512],
                             gps[:, gq * 512:(gq + 1) * 512],
                             cmb_g[par0][:, st, e:e + 1], None, ALU.mult),
                         waits=[("ch_pe", gmm)], chain=True)
                    gacc_done[gq] = r.val("ch_dve")
                sdone = r.val("ch_dve")
                r.op("gpsimd", lambda g, e=e, st=st, sh=sh:
                     g.indirect_dma_start(
                         out=scr_ge, out_offset=bass.IndirectOffsetOnAxis(
                             gdst_sb[:, e * CT + st:e * CT + st + 1], 0),
                         in_=ge_stage[sh][:], in_offset=None),
                     waits=[("ch_dve", sdone), ("d_ge", ge_zeroed),
                            ("d_gb", r.val("d_gb"))],
                     incs=[("d_gb", 16)])
                gb_dma[sh] = r.val("d_gb")
            ms = r.milestone("tensor")
            slot_free[s1] = ms
            slot_free[s2] = ms

        gacc_done = {}
        r.wait("sync", [("d_wc", wc_done)])
        gather_e(0, [("ch_dve", xz)])
        gather_e(1, [])
        transpose_e(0)
        wl(0, scr_w1[0, :, :, 0:1024])
        wl(1, scr_w3[0, :, :, 0:1024])
        wl(2, scr_w1[0, :, :, 1024:2048])
        for e in range(NE):
            h_phase_e(e, 0, 1, 0)
            wl(0, scr_w3[e, :, :, 1024:2048])
            h_phase_e(e, 0, 3, 1)
            wl(1, scr_w2[e, :, 0:NCH, :])
            h_phase_e(e, 1, 1, 2)
            wl(2, scr_w2[e, :, NCH:2 * NCH, :])
            h_phase_e(e, 1, 3, 0)
            if e + 1 < NE:
                wl(0, scr_w1[e + 1, :, :, 0:1024])
                transpose_e(e + 1)
            g_phase_e(e, 1, 2)
            if e + 2 < NE:
                gather_e(e + 2, [("ch_pe", tr_done[e])])
            if e + 1 < NE:
                wl(1, scr_w3[e + 1, :, :, 0:1024])
                wl(2, scr_w1[e + 1, :, :, 1024:2048])

        r.wait("sync", [(x, r.val(x)) for x in
                        ("d_ws0", "d_ws1", "d_ws2", "d_gi", "d_xe0",
                         "d_xe1", "d_gb", "d_wc")])
        blocks.append(("P3", dict(r.ops)))
        r.ops = {e: [] for e in Rec.ENGS}
        close("xe_tok0", "xe_tok1", "cmb_g0",
              "cmb_g1", "xe_feat0", "xe_feat1", "h_buf", "wsl0", "wsl1",
              "wsl2", "ge_stage0", "ge_stage1", "pa", "gps", "pst3")
    else:
        combine = sb("combine", [128, OWN_QT * NE], F32)
        g_sb = sb("g_sb", [128, OWN_QT, D], F32)
        h_buf = sb("h_buf", [128, HID // 128, TH], BF16)
        wsl = [sb(f"wsl{i}", [128, NCH, 1024], BF16) for i in range(3)]
        stmp = sb("stmp", [128, 1024], BF16)
        gm = sb("gm", [128, 4 * NE], F32)

        pa = psum("pa", [128, 2048], F32)
        gps = psum("gps", [128, 1024], F32)

        r.op("vector", lambda v: v.memset(g_sb[:].rearrange("p t d -> p (t d)"),
                                          0.0), chain=True)

        for tt in range(OWN_QT):
            sc = scores[:, tt * NE:(tt + 1) * NE]
            r.op("vector", lambda v, sc=sc: v.reduce_max(
                sm[:, 0:1], sc, axis=AX.X),
                waits=[("ch_act", r.val("ch_act"))], chain=True)
            r.op("vector", lambda v, sc=sc: v.tensor_scalar(
                gm[:, 0:NE], sc, sm[:, 0:1], None, ALU.is_equal),
                chain=True)
            r.op("vector", lambda v, sc=sc: v.scalar_tensor_tensor(
                gm[:, NE:2 * NE], gm[:, 0:NE], -1e9, sc,
                ALU.mult, ALU.add), chain=True)
            r.op("vector", lambda v: v.reduce_max(
                sm[:, 1:2], gm[:, NE:2 * NE], axis=AX.X), chain=True)
            r.op("vector", lambda v: v.tensor_scalar(
                gm[:, 2 * NE:3 * NE], gm[:, NE:2 * NE], sm[:, 1:2], None,
                ALU.is_equal), chain=True)
            r.op("vector", lambda v: v.tensor_sub(
                sm[:, 2:3], sm[:, 0:1], sm[:, 1:2]), chain=True)
            dcol = r.val("ch_dve")
            r.op("scalar", lambda s: s.activation(
                sm[:, 3:4], sm[:, 2:3], AF.Sigmoid),
                waits=[("ch_dve", dcol)], chain=True)
            r.op("scalar", lambda s: s.activation(
                sm[:, 4:5], sm[:, 2:3], AF.Sigmoid, scale=-1.0), chain=True)
            sig = r.val("ch_act")
            r.op("vector", lambda v: v.tensor_scalar(
                gm[:, 3 * NE:4 * NE], gm[:, 0:NE], sm[:, 3:4], None, ALU.mult),
                waits=[("ch_act", sig)], chain=True)
            r.op("vector", lambda v, tt=tt: v.scalar_tensor_tensor(
                combine[:, tt * NE:(tt + 1) * NE], gm[:, 2 * NE:3 * NE],
                sm[:, 4:5], gm[:, 3 * NE:4 * NE], ALU.mult, ALU.add),
                chain=True)
        gate_done = r.val("ch_dve")

        hmul_done = {}
        gacc_done = {}
        wsl_dma = [0, 0, 0]
        slot_free = [0, 0, 0]

        def wl(slot, src_ap):
            r.op("gpsimd", lambda g, slot=slot, src_ap=src_ap: g.dma_start(
                out=wsl[slot][:], in_=src_ap),
                waits=[("ch_pe", slot_free[slot])], incs=[(f"d_ws{slot}", 16)])
            wsl_dma[slot] = r.val(f"d_ws{slot}")

        def h_phase(e, th, wh, kind, slot):
            tok0 = th * TH
            r.wait("tensor", [(f"d_ws{slot}", wsl_dma[slot]),
                              ("ch_dve", gate_done)])
            for hc8 in range(NCH):
                hc = wh * NCH + hc8
                for tg in range(TH // TGS):
                    par = (hc8 * (TH // TGS) + tg) % 2
                    r.wait("tensor", [("ch_act", hsil_done.get(par, 0)),
                                      ("ch_dve", hmul_done.get(par, 0))])
                    for c in range(NCH):
                        r.op("tensor", lambda te, c=c, hc8=hc8, tg=tg, tok0=tok0,
                             par=par, slot=slot: te.matmul(
                                 pa[:, par * 1024:par * 1024 + TGS],
                                 wsl[slot][:, c, hc8 * 128:(hc8 + 1) * 128],
                                 xn_own[:, c, tok0 + tg * TGS:
                                        tok0 + (tg + 1) * TGS],
                                 start=(c == 0), stop=(c == NCH - 1)))
                    hmm = r.milestone("tensor")
                    if kind == 1:
                        r.op("scalar", lambda s, hc=hc, tg=tg, par=par:
                             s.activation(
                                 h_buf[:, hc, tg * TGS:(tg + 1) * TGS],
                                 pa[:, par * 1024:par * 1024 + TGS], AF.Silu),
                             waits=[("ch_pe", hmm)], chain=True)
                        hsil_done[par] = r.val("ch_act")
                    else:
                        r.op("vector", lambda v, hc=hc, tg=tg, par=par:
                             v.tensor_tensor(
                                 h_buf[:, hc, tg * TGS:(tg + 1) * TGS],
                                 h_buf[:, hc, tg * TGS:(tg + 1) * TGS],
                                 pa[:, par * 1024:par * 1024 + TGS],
                                 ALU.mult),
                             waits=[("ch_pe", hmm)], chain=True)
                        hmul_done[par] = r.val("ch_dve")
            slot_free[slot] = r.milestone("tensor")

        def g_phase(e, th, s1, s2):
            hdone = r.val("ch_dve")
            r.wait("tensor", [(f"d_ws{s1}", wsl_dma[s1]),
                              (f"d_ws{s2}", wsl_dma[s2]), ("ch_dve", hdone)])
            for tt8 in range(TH // 128):
                tt = th * (TH // 128) + tt8
                for mh in range(2):
                    gq = (tt8 * 2 + mh) % 2
                    r.wait("tensor", [("ch_dve", gacc_done.get(gq, 0))])
                    for hc in range(HID // 128):
                        slot = s1 if hc < NCH else s2
                        r.op("tensor", lambda te, tt8=tt8, mh=mh, hc=hc,
                             gq=gq, slot=slot: te.matmul(
                                 gps[:, gq * 512:(gq + 1) * 512],
                                 h_buf[:, hc, tt8 * 128:(tt8 + 1) * 128],
                                 wsl[slot][:, hc % NCH, mh * 512:(mh + 1) * 512],
                                 start=(hc == 0), stop=(hc == HID // 128 - 1)))
                    gmm = r.milestone("tensor")
                    r.op("vector", lambda v, tt=tt, mh=mh, e=e, gq=gq:
                         v.scalar_tensor_tensor(
                             g_sb[:, tt, mh * 512:(mh + 1) * 512],
                             gps[:, gq * 512:(gq + 1) * 512],
                             combine[:, tt * NE + e:tt * NE + e + 1],
                             g_sb[:, tt, mh * 512:(mh + 1) * 512],
                             ALU.mult, ALU.add),
                         waits=[("ch_pe", gmm)], chain=True)
                    gacc_done[gq] = r.val("ch_dve")
            ms = r.milestone("tensor")
            slot_free[s1] = ms
            slot_free[s2] = ms

        hsil_done = {}
        wl(0, scr_w1[0, :, :, 0:1024])
        wl(1, scr_w3[0, :, :, 0:1024])
        wl(2, scr_w1[0, :, :, 1024:2048])
        for e in range(NE):
            for th in range(2):
                h_phase(e, th, 0, 1, 0)
                wl(0, scr_w3[e, :, :, 1024:2048])
                h_phase(e, th, 0, 3, 1)
                wl(1, scr_w2[e, :, 0:NCH, :])
                h_phase(e, th, 1, 1, 2)
                wl(2, scr_w2[e, :, NCH:2 * NCH, :])
                h_phase(e, th, 1, 3, 0)
                ne, nth = (e, th + 1) if th == 0 else (e + 1, 0)
                if ne < NE:
                    wl(0, scr_w1[ne, :, :, 0:1024])
                g_phase(e, th, 1, 2)
                if ne < NE:
                    wl(1, scr_w3[ne, :, :, 0:1024])
                    wl(2, scr_w1[ne, :, :, 1024:2048])

        r.wait("sync", [(x, r.val(x)) for x in
                        ("d_ws0", "d_ws1", "d_ws2")])
        blocks.append(("P3", dict(r.ops)))
        r.ops = {e: [] for e in Rec.ENGS}

        close("h_buf", "wsl0", "wsl1", "wsl2", "stmp", "gm", "pa", "gps",
              "combine", "g_sb")

    xo_buf = [sb(f"xo_buf{i}", [128, D], F32) for i in range(2)]
    sv_buf = [sb(f"sv_buf{i}", [128, D], BF16) for i in range(2)]
    sg_buf = sb("sg_buf", [128, D], F32)
    o_stage = [sb(f"o_stage{i}", [128, D], F32) for i in range(2)]
    if F_SPARSE:
        ge_buf = [sb(f"ge_buf{i}", [128, 2, D], F32) for i in range(2)]
    out_dma = {}
    fin_done = {}
    gadd_done = {}
    dgb_final = r.val("d_gb") if F_SPARSE else 0
    for qt in range(OWN_QT):
        pi = qt % 2
        lw = []
        if qt >= 2:
            lw.append(("ch_dve", fin_done[qt - 2]))
        r.op("sync", lambda sy, qt=qt, pi=pi: sy.dma_start(
            out=xo_buf[pi][:], in_=xo_in[qt * 128:(qt + 1) * 128, :]),
            waits=lw, incs=[(f"d_x{pi}", 16)])
        r.op("sync", lambda sy, qt=qt, pi=pi: sy.dma_start(
            out=sv_buf[pi][:], in_=scr_s[qt, :, :]),
            waits=[], incs=[(f"d_x{pi}", 16)])
        ld = r.val(f"d_x{pi}")
        if F_SPARSE:
            gw = ([("ch_dve", gadd_done[qt - 2])] if qt >= 2
                  else [("d_gb", dgb_final)])
            r.op("gpsimd", lambda g, qt=qt, pi=pi: g.dma_start(
                out=ge_buf[pi][:, 0, :],
                in_=scr_ge[qt * 128:(qt + 1) * 128, :]),
                waits=gw, incs=[(f"d_g{pi}", 16)])
            r.op("gpsimd", lambda g, qt=qt, pi=pi: g.dma_start(
                out=ge_buf[pi][:, 1, :],
                in_=scr_ge[OWNQ + qt * 128:OWNQ + (qt + 1) * 128, :]),
                incs=[(f"d_g{pi}", 16)])
            ldg = r.val(f"d_g{pi}")
            r.op("vector", lambda v, pi=pi: v.tensor_add(
                ge_buf[pi][:, 0, :], ge_buf[pi][:, 0, :],
                ge_buf[pi][:, 1, :]),
                waits=[(f"d_g{pi}", ldg)], chain=True)
            gadd_done[qt] = r.val("ch_dve")
            r.op("scalar", lambda s, pi=pi: s.activation(
                sg_buf[:], ge_buf[pi][:, 0, :], AF.Silu),
                waits=[("ch_dve", gadd_done[qt])], chain=True)
        else:
            r.op("scalar", lambda s, qt=qt: s.activation(
                sg_buf[:], g_sb[:, qt, :], AF.Silu), chain=True)
        sg = r.val("ch_act")
        ow = [("ch_act", sg), (f"d_x{pi}", ld)]
        if qt >= 2:
            ow.append((f"d_o{pi}", out_dma[qt - 2]))
        r.op("vector", lambda v, pi=pi: v.tensor_tensor(
            o_stage[pi][:], sv_buf[pi][:], sg_buf[:], ALU.mult),
            waits=ow, chain=True)
        r.op("vector", lambda v, pi=pi: v.tensor_add(
            o_stage[pi][:], o_stage[pi][:], xo_buf[pi][:]), chain=True)
        fin_done[qt] = r.val("ch_dve")
        r.op("scalar", lambda s, qt=qt, pi=pi: s.dma_start(
            out=out_t[qt * 128:(qt + 1) * 128, :], in_=o_stage[pi][:]),
            waits=[("ch_dve", fin_done[qt])], incs=[(f"d_o{pi}", 16)])
        out_dma[qt] = r.val(f"d_o{pi}")

    r.wait("sync", [(x, r.val(x)) for x in
                    ("d_o0", "d_o1", "d_x0", "d_x1", "d_g0", "d_g1")
                    if r.val(x) > 0])
    blocks.append(("P4", dict(r.ops)))
    r.ops = {e: [] for e in Rec.ENGS}

    semnames = list(r.cnt.keys())
    sems = {}
    semctx = []
    for s in semnames:
        cm = nc.semaphore(s)
        sems[s] = cm.__enter__()
        semctx.append(cm)

    for name, ops in (blocks if nblocks is None else blocks[:nblocks]):
        rb = Rec()
        rb.ops = ops
        emit_block(nc, rb, sems, name)

    return nc


def emit_block(nc, rec, sems, name):
    todo = {e: rec.ops[e] for e in Rec.ENGS if rec.ops[e]}
    if not todo:
        return
    with nc.Block(name) as blk:
        for ename, oplist in todo.items():
            def mk(oplist):
                def body(eng):
                    for waits, fn, incs in oplist:
                        for s, v in waits:
                            if v > 0:
                                eng.wait_ge(sems[s], v)
                        if fn is None:
                            continue
                        inst = fn(eng)
                        for s, n in incs:
                            inst.then_inc(sems[s], n)
                return body
            getattr(blk, ename)(mk(oplist))


def _host_inputs(core, X, Wq, Wk, Wv, Wo, bQ, bK, bV, bO, Wg, W1, W3, W2,
                 seq):
    NT = seq // 128
    NM = NT // 8
    b, j = core // 4, core % 4
    qtiles = []
    for m in range(NM):
        qtiles += [8 * m + j, 8 * m + 7 - j]

    xb = np.ascontiguousarray(X[b])
    x_own = np.ascontiguousarray(
        np.concatenate([xb[t * 128:(t + 1) * 128] for t in qtiles], axis=0))

    def packQK(Wt):
        w4 = Wt.reshape(NP, 2, NCH, 128, DH)
        return np.ascontiguousarray(w4.transpose(3, 0, 2, 1, 4)
                                    .reshape(128, NP, NCH, 128))

    wq = packQK(Wq)
    wk = packQK(Wk)
    wv = np.ascontiguousarray(
        Wv.reshape(H, NCH, 128, DH).transpose(2, 1, 0, 3)
        .reshape(128, NCH, H * DH))
    bq = np.ascontiguousarray(
        bQ.reshape(NP, 2, DH).transpose(1, 2, 0).reshape(128, NP))
    bk = np.ascontiguousarray(
        bK.reshape(NP, 2, DH).transpose(1, 2, 0).reshape(128, NP))
    bv = np.ascontiguousarray(
        bV.reshape(NP, 2, DH).transpose(1, 2, 0).reshape(128, NP))
    wo = np.ascontiguousarray(
        Wo.reshape(NCH, 128, D).transpose(1, 0, 2))
    bo = np.ascontiguousarray(bO.reshape(1, D))
    wg = np.ascontiguousarray(
        Wg.reshape(NCH, 128, NE).transpose(1, 0, 2))
    w1 = np.ascontiguousarray(
        W1.reshape(NE, NCH, 128, HID).transpose(0, 2, 1, 3))
    w3 = np.ascontiguousarray(
        W3.reshape(NE, NCH, 128, HID).transpose(0, 2, 1, 3))
    w2 = np.ascontiguousarray(
        W2.reshape(NE, HID // 128, 128, D).transpose(0, 2, 1, 3))

    kk = np.arange(128)[:, None]
    qq = np.arange(128)[None, :]
    pmask = np.zeros((NM, 12, 128, 128), dtype=np.float32)
    for m in range(NM):
        t0 = 8 * m + j
        t1 = 8 * m + 7 - j
        for i in range(8):
            kt = 8 * m + i
            if m == 0:
                pmask[m, i] = 0.0
            else:
                pmask[m, i] = (128 * kt + kk <= 128 * t0 + qq)
        for i in range(4):
            kt = 8 * m + 4 + i
            pmask[m, 8 + i] = (128 * kt + kk <= 128 * t1 + qq)
    pmask = np.ascontiguousarray(
        pmask.transpose(2, 0, 1, 3).reshape(128, NM * 12 * 128)
    ).astype(ml_dtypes.bfloat16)

    kk0 = np.arange(Q0KT * 128)[None, :]
    qq0 = np.arange(128)[:, None]
    pmask0 = np.where(kk0 <= 128 * j + qq0, 0.0, -30000.0).astype(np.float32)

    return dict(x=xb, x_own=x_own, wq=wq, wk=wk, wv=wv, bq=bq, bk=bk, bv=bv,
                wo=wo, bo=bo, wg=wg, w1=w1, w3=w3, w2=w2,
                pmask=pmask, pmask0=pmask0), qtiles


def kernel(L0_attn_residual, L0_attn_W_Q, L0_attn_W_K, L0_attn_W_V,
           L0_attn_W_O, L0_attn_b_Q, L0_attn_b_K, L0_attn_b_V, L0_attn_b_O,
           L0_ffn_Wg, L0_ffn_W1, L0_ffn_W3, L0_ffn_W2):
    X = np.asarray(L0_attn_residual, dtype=np.float32)
    seq = X.shape[1]
    nc = build(seq)

    in_maps = []
    all_qtiles = []
    for c in range(8):
        m, qt = _host_inputs(
            c, X, np.asarray(L0_attn_W_Q, np.float32),
            np.asarray(L0_attn_W_K, np.float32),
            np.asarray(L0_attn_W_V, np.float32),
            np.asarray(L0_attn_W_O, np.float32),
            np.asarray(L0_attn_b_Q, np.float32),
            np.asarray(L0_attn_b_K, np.float32),
            np.asarray(L0_attn_b_V, np.float32),
            np.asarray(L0_attn_b_O, np.float32),
            np.asarray(L0_ffn_Wg, np.float32),
            np.asarray(L0_ffn_W1, np.float32),
            np.asarray(L0_ffn_W3, np.float32),
            np.asarray(L0_ffn_W2, np.float32), seq)
        in_maps.append(m)
        all_qtiles.append(qt)

    res = run_bass_kernel_spmd(nc, in_maps, core_ids=list(range(8)))

    out = np.empty((B, seq, D), dtype=np.float32)
    for c in range(8):
        b = c // 4
        o = res.results[c]["out"]
        for i, t in enumerate(all_qtiles[c]):
            out[b, t * 128:(t + 1) * 128, :] = o[i * 128:(i + 1) * 128, :]
    return out
